# version 25
# baseline (speedup 1.0000x reference)
import numpy as np
import concourse.bass as bass
import concourse.mybir as mybir
from concourse.bass_utils import run_bass_kernel_spmd
from concourse.hw_specs import get_activation_tables

F32 = mybir.dt.float32
BF16 = mybir.dt.bfloat16
AF = mybir.ActivationFunctionType
ALU = mybir.AluOpType
AX = mybir.AxisListType

S = 2048
D = 1024
I = 2048
NE = 8
NTT = S // 128
NHC = D // 128
NIC = I // 128
NEG_BIG = -30000.0


class Ctr:
    def __init__(self, nc, name):
        self.h = nc.alloc_semaphore(name)
        self.n = 0


def build_program(upto=None):
    nc = bass.Bass("TRN2", target_bir_lowering=False)

    x_d = nc.dram_tensor("x", [S, D], F32, kind="ExternalInput").ap()
    wq_d = nc.dram_tensor("wq", [D, D], F32, kind="ExternalInput").ap()
    wk_d = nc.dram_tensor("wk", [D, D], F32, kind="ExternalInput").ap()
    wv_d = nc.dram_tensor("wv", [D, D], F32, kind="ExternalInput").ap()
    sgwT_d = nc.dram_tensor("sgwT", [D, I], F32, kind="ExternalInput").ap()
    suwT_d = nc.dram_tensor("suwT", [D, I], F32, kind="ExternalInput").ap()
    sdwT_d = nc.dram_tensor("sdwT", [I, D], F32, kind="ExternalInput").ap()
    gw_d = nc.dram_tensor("gw", [NE, D, I], F32, kind="ExternalInput").ap()
    uw_d = nc.dram_tensor("uw", [NE, D, I], F32, kind="ExternalInput").ap()
    dw_d = nc.dram_tensor("dw", [NE, I, D], F32, kind="ExternalInput").ap()
    rwT_d = nc.dram_tensor("rwT", [D, NE], F32, kind="ExternalInput").ap()
    rbias_d = nc.dram_tensor("rbias", [128, NE], F32, kind="ExternalInput").ap()
    ident_d = nc.dram_tensor("ident", [128, 128], F32, kind="ExternalInput").ap()
    dmask_d = nc.dram_tensor("dmask", [128, 128], F32, kind="ExternalInput").ap()
    out_d = nc.dram_tensor("out", [S, D], F32, kind="ExternalOutput").ap()

    R2 = nc.alloc_sbuf_tensor("R2", [128, NTT, D], F32)
    XQ = nc.alloc_sbuf_tensor("XQ", [128, NHC, S], BF16)
    KBUF = nc.alloc_sbuf_tensor("KBUF", [128, 16384], BF16)
    VBUF = nc.alloc_sbuf_tensor("VBUF", [128, 16384], BF16)
    WBUF = nc.alloc_sbuf_tensor("WBUF", [128, 8192], BF16)
    sS = nc.alloc_sbuf_tensor("sS", [128, 5760], BF16)
    sP = nc.alloc_sbuf_tensor("sP", [128, 2, S], BF16)
    sPT = nc.alloc_sbuf_tensor("sPT", [128, 8, 128], BF16)
    xlntm = nc.alloc_sbuf_tensor("xlntm", [128, 2, D], BF16)
    gscr = nc.alloc_sbuf_tensor("gscr", [128, 2, 512], BF16)
    ident = nc.alloc_sbuf_tensor("identsb", [128, 128], BF16)
    dmask = nc.alloc_sbuf_tensor("dmasksb", [128, 128], BF16)
    rwTs = nc.alloc_sbuf_tensor("rwTs", [128, NHC, NE], BF16)
    rbias = nc.alloc_sbuf_tensor("rbiassb", [128, NE], F32)
    lgt = nc.alloc_sbuf_tensor("lgt", [128, NTT, NE], F32)
    prb = nc.alloc_sbuf_tensor("prb", [128, NTT, NE], F32)
    eqb = nc.alloc_sbuf_tensor("eqb", [128, NTT, NE], F32)
    selb = nc.alloc_sbuf_tensor("selb", [128, NTT, NE], F32)
    rst = nc.alloc_sbuf_tensor("rst", [128, 4, NTT], F32)
    wrt = nc.alloc_sbuf_tensor("wrt", [128, NTT, NE], F32)
    qst = nc.alloc_sbuf_tensor("qst", [128, 4, NTT], F32)
    lst = nc.alloc_sbuf_tensor("lst", [128, 5, NTT], F32)
    cst = nc.alloc_sbuf_tensor("cst", [128, 4], F32)
    qcm = nc.alloc_sbuf_tensor("qcm", [128, 3, 4], F32)

    _sS_off = (0, 2048, 3840)

    def sSs(qi, a, b):
        o = _sS_off[qi % 3]
        return sS[:, o + a:o + b]

    kfm = KBUF[:, :].rearrange("p (c t) -> p c t", c=NHC)
    act = KBUF[:, :].rearrange("p (c t) -> p c t", c=NIC)
    vtm = VBUF[:, :].rearrange("p (c t) -> p c t", c=NTT)
    dwe = VBUF[:, :].rearrange("p (c t) -> p c t", c=NIC)
    wqkv = WBUF[:, :].rearrange("p (c t) -> p c t", c=NHC)
    wgrp = WBUF[:, :].rearrange("p (b g c t) -> p b g c t", b=2, g=2, c=NHC)

    psA = nc.alloc_psum_tensor("psA", [128, 512], F32)
    psB = nc.alloc_psum_tensor("psB", [128, 512], F32)
    psC = nc.alloc_psum_tensor("psC", [128, 512], F32)
    psD = nc.alloc_psum_tensor("psD", [128, 512], F32)
    psE = nc.alloc_psum_tensor("psE", [128, 512], F32)
    psF = nc.alloc_psum_tensor("psF", [128, 512], F32)
    ptA = nc.alloc_psum_tensor("ptA", [128, 256], BF16)
    ptB = nc.alloc_psum_tensor("ptB", [128, 256], BF16)

    ptb = [ptA[:, :], ptB[:, :],
           psE[:, :].bitcast(BF16)[:, 0:256], psF[:, :].bitcast(BF16)[:, 0:256]]

    dWc = Ctr(nc, "dWc")
    dWw = Ctr(nc, "dWw")
    dWp = Ctr(nc, "dWp")
    dWg = [Ctr(nc, "dWg0"), Ctr(nc, "dWg1")]
    dWu = [Ctr(nc, "dWu0"), Ctr(nc, "dWu1")]
    dWd = Ctr(nc, "dWd")
    dXc = [Ctr(nc, "dXc%d" % j) for j in range(4)]
    dO = Ctr(nc, "dO")
    tMM = Ctr(nc, "tMM")
    tTR = Ctr(nc, "tTR")
    vA = Ctr(nc, "vA")
    sA = Ctr(nc, "sA")

    prog = {k: [] for k in ("sync", "gpsimd", "tensor", "vector", "scalar")}
    dbg = {"stopped": False}

    def em(engine, fn):
        if not dbg["stopped"]:
            prog[engine].append(fn)

    def wait(engine, ctr, tok):
        if tok > 0:
            em(engine, lambda eng, c=ctr, t=tok: eng.wait_ge(c.h, t))

    def inc(engine, build, ctr, k=1):
        ctr.n += k
        tok = ctr.n
        em(engine, lambda eng, b=build, c=ctr, kk=k: b(eng).then_inc(c.h, kk))
        return tok

    rel = {}

    def bwait(engine, bank):
        r = rel.get(id(bank))
        if r is not None:
            wait(engine, r[0], r[1])

    def bset(bank, ctr, tok):
        rel[id(bank)] = (ctr, tok)

    def dump_and_stop():
        wait("sync", vA, vA.n)
        wait("sync", sA, sA.n)
        wait("sync", tMM, tMM.n)
        wait("sync", tTR, tTR.n)
        n = 0
        for name, buf, shape, dt in (
                ("d_r2", R2, [128, NTT * D], F32),
                ("d_xq", XQ, [128, NHC * S], BF16),
                ("d_k", KBUF, [128, 16384], BF16),
                ("d_v", VBUF, [128, 16384], BF16),
                ("d_qst", qst, [128, 4 * NTT], F32),
                ("d_lst", lst, [128, 5 * NTT], F32),
                ("d_wrt", wrt, [128, NTT * NE], F32),
                ("d_prb", prb, [128, NTT * NE], F32),
        ):
            dr = nc.dram_tensor(name, shape, dt, kind="ExternalOutput").ap()
            ap = buf[:, :] if len(buf.shape) == 2 else buf[:, :, :].rearrange(
                "p a b -> p (a b)")
            inc("sync", lambda e, dr=dr, ap=ap: e.dma_start(out=dr, in_=ap),
                dO, 16)
            n += 1
        wait("sync", dO, 16 * n)
        dbg["stopped"] = True

    t_xc = []
    for j in range(4):
        t_xc.append((dXc[j], inc("sync", lambda e, j=j: e.dma_start(
            out=R2[:, 4 * j:4 * (j + 1), :],
            in_=x_d[512 * j:512 * (j + 1), :]
            .rearrange("(c p) h -> p c h", p=128)), dXc[j], 16)))
    inc("gpsimd", lambda e: e.dma_start(out=dmask[:, :], in_=dmask_d), dWc, 16)
    inc("gpsimd", lambda e: e.dma_start(out=ident[:, :], in_=ident_d), dWc, 16)
    inc("gpsimd", lambda e: e.dma_start(out=rbias[:, :], in_=rbias_d), dWc, 16)
    t_rwT = inc("gpsimd", lambda e: e.dma_start(
        out=rwTs[:, :, :], in_=rwT_d.rearrange("(c p) e -> p c e", p=128)), dWc, 16)
    t_dmask = t_ident = t_rbias = t_rwT
    t_wk = inc("gpsimd", lambda e: e.dma_start(
        out=wqkv[:, :, :], in_=wk_d.rearrange("(c p) e -> p c e", p=128)), dWw, 16)
    inc("gpsimd", lambda e: e.dma_start(
        out=sS[:, 0:4096].rearrange("p (c t) -> p c t", c=4),
        in_=wv_d[0:512, :].rearrange("(c p) e -> p c e", p=128)), dWp, 16)
    t_wv_park = inc("gpsimd", lambda e: e.dma_start(
        out=sP[:, :, :].rearrange("p a b -> p (a b)").rearrange(
            "p (c t) -> p c t", c=4),
        in_=wv_d[512:1024, :].rearrange("(c p) e -> p c e", p=128)), dWp, 16)

    em("vector", lambda e: e.memset(cst[:, 0:1], 1e-5))
    em("vector", lambda e: e.memset(cst[:, 1:2], -2.0))


    lnsq = [0] * NTT
    lnsq_prev = [0]
    lntr = [0] * NTT

    def ln_stats(tt, v_pre=0, v_dx=None):
        if v_dx is not None:
            wait("vector", v_dx[0], v_dx[1])
        wait("vector", vA, v_pre)
        r_tok = inc("vector", lambda e, tt=tt: e.reduce_sum(
            lst[:, 0, tt:tt + 1], R2[:, tt, :], axis=AX.X, negate=True), vA)
        wait("vector", vA, r_tok)
        m_tok = inc("vector", lambda e, tt=tt: e.tensor_scalar(
            lst[:, 1, tt:tt + 1], lst[:, 0, tt:tt + 1], 1.0 / D, None,
            op0=ALU.mult), vA)
        wait("scalar", vA, m_tok)
        if v_dx is not None:
            wait("scalar", v_dx[0], v_dx[1])
        wait("scalar", sA, lnsq_prev[0])
        sq_tok = inc("scalar", lambda e, tt=tt: e.activation(
            gscr[:, :, :].rearrange("p a b -> p (a b)"), R2[:, tt, :],
            AF.Square, bias=lst[:, 1, tt:tt + 1],
            scale=1.0, accum_out=lst[:, 2, tt:tt + 1]), sA)
        lnsq_prev[0] = sq_tok
        lnsq[tt] = sq_tok
        return sq_tok

    lnap = [0] * NTT

    def ln_fin_a(tt):
        wait("scalar", sA, lnsq[tt])
        l1 = inc("scalar", lambda e, tt=tt: e.activation(
            lst[:, 3, tt:tt + 1], lst[:, 2, tt:tt + 1], AF.Ln,
            bias=cst[:, 0:1], scale=1.0 / D), sA)
        wait("scalar", sA, l1)
        l2 = inc("scalar", lambda e, tt=tt: e.activation(
            lst[:, 4, tt:tt + 1], lst[:, 3, tt:tt + 1], AF.Exp,
            scale=-0.5), sA)
        wait("vector", sA, l2)
        if tt >= 2:
            wait("vector", tTR, lntr[tt - 2])
        lnap[tt] = inc("vector", lambda e, tt=tt: e.tensor_scalar(
            xlntm[:, tt % 2, :], R2[:, tt, :], lst[:, 1, tt:tt + 1],
            lst[:, 4, tt:tt + 1], op0=ALU.add, op1=ALU.mult), vA)

    def ln_fin_b(tt):
        wait("tensor", vA, lnap[tt])
        wait("tensor", dWc, t_ident)
        xq_tok = 0
        for hp in range(NHC // 2):
            pt = ptb[hp % 4]
            bwait("tensor", pt)
            tr_tok = 0
            for j in range(2):
                hc = 2 * hp + j
                tr_tok = inc("tensor", lambda e, tt=tt, hc=hc, j=j, pt=pt:
                             e.transpose(
                                 pt[:, j * 128:(j + 1) * 128],
                                 xlntm[:, tt % 2, hc * 128:(hc + 1) * 128],
                                 ident[:, :]), tTR)
            lntr[tt] = tr_tok
            wait("scalar", tTR, tr_tok)
            xq_tok = inc("scalar", lambda e, tt=tt, hp=hp, pt=pt: e.copy(
                XQ[:, 2 * hp:2 * hp + 2, tt * 128:(tt + 1) * 128],
                pt[:, :].rearrange("p (a b) -> p a b", a=2)), sA)
            bset(pt, sA, xq_tok)
        return xq_tok

    def ln_finish():
        xq_tok = 0
        for tt in range(NTT):
            ln_fin_a(tt)
            xq_tok = ln_fin_b(tt)
        return xq_tok

    qkv_banks = (psA, psB, psC, psD)
    bi = 0
    wait("tensor", dWw, t_wk)
    xq_ready_s = 0
    for ts in range(4):
        for tt4 in range(4):
            tt = 4 * ts + tt4
            ln_stats(tt, v_pre=0, v_dx=t_xc[tt // 4])
        for tt4 in range(4):
            tt = 4 * ts + tt4
            ln_fin_a(tt)
            xq_ready_s = ln_fin_b(tt)
        wait("tensor", sA, xq_ready_s)
        for ec in range(NHC):
            b = qkv_banks[bi % 4]
            bwait("tensor", b)
            tok = 0
            for d in range(NHC):
                tok = inc("tensor", lambda e, d=d, ec=ec, ts=ts, b=b: e.matmul(
                    b[:, :], wqkv[:, d, ec * 128:(ec + 1) * 128],
                    XQ[:, d, ts * 512:(ts + 1) * 512],
                    start=(d == 0), stop=(d == NHC - 1)), tMM)
            wait("scalar", tMM, tok)
            ev = inc("scalar", lambda e, ec=ec, ts=ts, b=b: e.copy(
                kfm[:, ec, ts * 512:(ts + 1) * 512], b[:, :]), sA)
            bset(b, sA, ev)
            bi += 1
    t_k_mm = tMM.n

    wait("gpsimd", tMM, t_k_mm)
    t_wq = inc("gpsimd", lambda e: e.dma_start(
        out=wqkv[:, :, :], in_=wq_d.rearrange("(c p) e -> p c e", p=128)), dWw, 16)

    wait("tensor", dWp, t_wv_park)
    wv_lo = sS[:, 0:4096].rearrange("p (c t) -> p c t", c=4)
    wv_hi = sP[:, :, :].rearrange("p a b -> p (a b)").rearrange(
        "p (c t) -> p c t", c=4)
    for tt in range(NTT):
        for eh in range(2):
            b = qkv_banks[bi % 4]
            bwait("tensor", b)
            tok = 0
            for d in range(NHC):
                wvsrc = wv_lo if d < 4 else wv_hi
                tok = inc("tensor", lambda e, d=d, tt=tt, eh=eh, b=b,
                          wvsrc=wvsrc: e.matmul(
                    b[:, :], XQ[:, d, tt * 128:(tt + 1) * 128],
                    wvsrc[:, d % 4, eh * 512:(eh + 1) * 512],
                    start=(d == 0), stop=(d == NHC - 1)), tMM)
            wait("scalar", tMM, tok)
            ev = inc("scalar", lambda e, tt=tt, eh=eh, b=b: e.copy(
                vtm[:, tt, eh * 512:(eh + 1) * 512], b[:, :]), sA)
            bset(b, sA, ev)
            bi += 1
    t_v_mm = tMM.n

    wait("tensor", dWw, t_wq)
    qsg = sS[:, 0:4096].rearrange(
        "p (s c t) -> p s c t", s=2, c=NHC)
    xqcopy_tok = [0] * 8
    ev_last = [0] * 8
    for t8 in range(8):
        if t8 >= 2:
            wait("scalar", sA, xqcopy_tok[t8 - 2])
        for b in qkv_banks:
            bwait("tensor", b)
        for bk in range(4):
            b = qkv_banks[bk]
            tok = 0
            for eci in range(2):
                ec = 2 * bk + eci
                half = eci * 256
                for d in range(NHC):
                    tok = inc("tensor", lambda e, d=d, ec=ec, t8=t8, b=b,
                              half=half: e.matmul(
                                  b[:, half:half + 256],
                                  wqkv[:, d, ec * 128:(ec + 1) * 128],
                                  XQ[:, d, t8 * 256:(t8 + 1) * 256],
                                  start=(d == 0), stop=(d == NHC - 1)), tMM)
            wait("scalar", tMM, tok)
            ev = inc("scalar", lambda e, bk=bk, t8=t8, b=b: e.copy(
                qsg[:, t8 % 2, 2 * bk:2 * bk + 2, :],
                b[:, :].rearrange("p (a b) -> p a b", a=2)), sA)
            bset(b, sA, ev)
            ev_last[t8] = ev
        if t8 > 0:
            wait("scalar", sA, ev_last[t8 - 1])
            xqcopy_tok[t8 - 1] = inc("scalar", lambda e, t8=t8: e.copy(
                XQ[:, :, (t8 - 1) * 256:t8 * 256],
                qsg[:, (t8 - 1) % 2, :, :]), sA)
    wait("scalar", sA, ev_last[7])
    t_q_xq = inc("scalar", lambda e: e.copy(
        XQ[:, :, 7 * 256:8 * 256], qsg[:, 7 % 2, :, :]), sA)
    xqcopy_tok[7] = t_q_xq
    t_q_mm = tMM.n
    if upto == "qkv":
        dump_and_stop()

    wait("tensor", dWc, t_dmask)
    wait("tensor", sA, t_q_xq)
    wait("vector", dWc, t_dmask)

    exp_tok = [0] * NTT
    trs_tok = [0] * NTT
    evs_tok = [0] * NTT
    evv_tok = [0] * NTT
    sc_bi = [0]

    def emit_scores(qi):
        nk = qi + 1
        ncol = nk * 128
        nsteps = (ncol + 511) // 512
        for ks in range(nsteps):
            w = min(512, ncol - ks * 512)
            b = (psA, psB)[sc_bi[0] % 2]
            sc_bi[0] += 1
            bwait("tensor", b)
            tok = 0
            for ec in range(NHC):
                tok = inc("tensor", lambda e, ec=ec, qi=qi, ks=ks, w=w, b=b:
                          e.matmul(b[:, 0:w],
                                   XQ[:, ec, qi * 128:(qi + 1) * 128],
                                   kfm[:, ec, ks * 512:ks * 512 + w],
                                   start=(ec == 0), stop=(ec == NHC - 1)), tMM)
            if ks < nsteps - 1:
                wait("scalar", tMM, tok)
                if qi >= 3 and ks == 0:
                    wait("scalar", sA, exp_tok[qi - 3])
                evs_tok[qi] = inc("scalar", lambda e, qi=qi, ks=ks, w=w, b=b:
                                  e.copy(sSs(qi, ks * 512, ks * 512 + w),
                                         b[:, 0:w]), sA)
                bset(b, sA, evs_tok[qi])
                wait("vector", sA, evs_tok[qi])
                inc("vector", lambda e, qi=qi, ks=ks, w=w: e.reduce_max(
                    qcm[:, qi % 3, ks:ks + 1],
                    sSs(qi, ks * 512, ks * 512 + w), axis=AX.X), vA)
            else:
                wait("vector", tMM, tok)
                if qi >= 3:
                    wait("vector", sA, exp_tok[qi - 3])
                if w > 128:
                    inc("vector", lambda e, qi=qi, ks=ks, w=w, b=b:
                        e.tensor_copy(sSs(qi, ks * 512, ks * 512 + w - 128),
                                      b[:, 0:w - 128]), vA)
                evv_tok[qi] = inc("vector", lambda e, qi=qi, ks=ks, w=w, b=b:
                                  e.tensor_tensor(
                                      sSs(qi, qi * 128, (qi + 1) * 128),
                                      b[:, w - 128:w], dmask[:, :],
                                      op=ALU.add), vA)
                bset(b, vA, evv_tok[qi])
                wait("vector", vA, evv_tok[qi])
                evv_tok[qi] = inc("vector", lambda e, qi=qi, ks=ks, w=w:
                                  e.reduce_max(
                                      qcm[:, qi % 3, ks:ks + 1],
                                      sSs(qi, ks * 512, ks * 512 + w),
                                      axis=AX.X), vA)

    def emit_softmax(qi):
        ncol = (qi + 1) * 128
        nsteps = (ncol + 511) // 512
        wait("vector", vA, evv_tok[qi])
        m_tok = inc("vector", lambda e, qi=qi, nsteps=nsteps: e.reduce_max(
            qst[:, 0, qi:qi + 1], qcm[:, qi % 3, 0:nsteps], axis=AX.X), vA)
        wait("vector", vA, m_tok)
        negb_tok = inc("vector", lambda e, qi=qi: e.tensor_scalar(
            qst[:, 1, qi:qi + 1], qst[:, 0, qi:qi + 1], -1.0 / 32.0, None,
            op0=ALU.mult), vA)
        wait("scalar", vA, negb_tok)
        if qi >= 2:
            wait("scalar", tTR, trs_tok[qi - 2])
        exp_tok[qi] = inc("scalar", lambda e, qi=qi, ncol=ncol: e.activation(
            sP[:, qi % 2, 0:ncol], sSs(qi, 0, ncol), AF.Exp,
            bias=qst[:, 1, qi:qi + 1], scale=1.0 / 32.0,
            accum_out=qst[:, 2, qi:qi + 1]), sA)

    def emit_ptpv(qi):
        nk = qi + 1
        npair = (nk + 1) // 2
        wait("tensor", sA, exp_tok[qi])
        bwait("tensor", psC)
        bwait("tensor", psD)
        pte = {}
        tr = []
        tok_pv = 0

        def do_pair(p):
            pt = ptb[p % 4]
            bwait("tensor", pt)
            cnt = 1 if (p == npair - 1 and nk % 2 == 1) else 2
            tr_tok = 0
            for j in range(cnt):
                kn = 2 * p + j
                tr_tok = inc("tensor", lambda e, qi=qi, kn=kn, j=j, pt=pt:
                             e.transpose(pt[:, j * 128:(j + 1) * 128],
                                         sP[:, qi % 2, kn * 128:(kn + 1) * 128],
                                         ident[:, :]), tTR)
            tr.append(tr_tok)
            wait("scalar", tTR, tr_tok)
            s = (2 * p) % 8
            pte[p] = inc("scalar", lambda e, p=p, pt=pt, s=s, cnt=cnt: e.copy(
                sPT[:, s:s + cnt, :].rearrange("p a b -> p (a b)"),
                pt[:, 0:cnt * 128]), sA)
            bset(pt, sA, pte[p])

        do_pair(0)
        if npair > 1:
            do_pair(1)
        for p in range(npair):
            if p + 2 < npair:
                do_pair(p + 2)
            wait("tensor", sA, pte[p])
            cnt = 1 if (p == npair - 1 and nk % 2 == 1) else 2
            for j in range(cnt):
                kc = 2 * p + j
                for eh, bb in ((0, psC), (1, psD)):
                    tok_pv = inc("tensor", lambda e, kc=kc, eh=eh, bb=bb, nk=nk:
                                 e.matmul(bb[:, :], sPT[:, kc % 8, :],
                                          vtm[:, kc, eh * 512:(eh + 1) * 512],
                                          start=(kc == 0),
                                          stop=(kc == nk - 1)), tMM)
        trs_tok[qi] = tr[-1]
        return tok_pv

    def emit_epilogue(qi, tok_pv):
        wait("vector", sA, exp_tok[qi])
        linv_tok = inc("vector", lambda e, qi=qi: e.reciprocal(
            qst[:, 3, qi:qi + 1], qst[:, 2, qi:qi + 1]), vA)
        wait("vector", vA, linv_tok)
        wait("vector", tMM, tok_pv)
        pv = 0
        for eh, bb in ((0, psC), (1, psD)):
            pv = inc("vector", lambda e, qi=qi, eh=eh, bb=bb:
                     e.scalar_tensor_tensor(
                         R2[:, qi, eh * 512:(eh + 1) * 512], bb[:, :],
                         qst[:, 3, qi:qi + 1],
                         R2[:, qi, eh * 512:(eh + 1) * 512],
                         op0=ALU.mult, op1=ALU.add), vA)
            bset(bb, vA, pv)
        return pv

    emit_scores(0)
    emit_scores(1)
    emit_softmax(0)
    for qi in range(NTT):
        if qi + 2 < NTT:
            emit_scores(qi + 2)
        tok_pv = emit_ptpv(qi)
        if qi + 1 < NTT:
            emit_softmax(qi + 1)
        pv = emit_epilogue(qi, tok_pv)
        ln_stats(qi, v_pre=pv)
        if qi >= 1:
            ln_fin_a(qi - 1)
        if qi >= 2:
            ln_fin_b(qi - 2)
    t_attn_v = vA.n
    if upto == "attn":
        dump_and_stop()

    ln_fin_a(NTT - 1)
    xq2_ready_s = ln_fin_b(NTT - 2)
    xq2_ready_s = ln_fin_b(NTT - 1)
    if upto == "ln2":
        dump_and_stop()

    wait("tensor", sA, xq2_ready_s)
    wait("tensor", dWc, t_rwT)
    bwait("tensor", psE)
    logit_prev = 0
    for tt in range(NTT):
        if logit_prev:
            wait("tensor", vA, logit_prev)
        tok = 0
        for hc in range(NHC):
            tok = inc("tensor", lambda e, tt=tt, hc=hc: e.matmul(
                psE[:, 0:NE], XQ[:, hc, tt * 128:(tt + 1) * 128], rwTs[:, hc, :],
                start=(hc == 0), stop=(hc == NHC - 1)), tMM)
        wait("vector", tMM, tok)
        if tt == 0:
            wait("vector", dWc, t_rbias)
        logit_prev = inc("vector", lambda e, tt=tt: e.tensor_tensor(
            lgt[:, tt, :], psE[:, 0:NE], rbias[:, :], op=ALU.add), vA)
    bset(psE, vA, logit_prev)
    wait("scalar", vA, logit_prev)
    sg_tok = inc("scalar", lambda e: e.activation(
        prb[:, :, :].rearrange("p a b -> p (a b)"),
        lgt[:, :, :].rearrange("p a b -> p (a b)"), AF.Sigmoid), sA)
    wait("vector", sA, sg_tok)
    t1 = inc("vector", lambda e: e.reduce_max(
        rst[:, 0, :].unsqueeze(-1), prb[:, :, :], axis=AX.X), vA)
    wait("vector", vA, t1)
    last = 0
    for tt in range(NTT):
        last = inc("vector", lambda e, tt=tt: e.tensor_scalar(
            eqb[:, tt, :], prb[:, tt, :], rst[:, 0, tt:tt + 1], None,
            op0=ALU.is_ge), vA)
    wait("vector", vA, last)
    for tt in range(NTT):
        last = inc("vector", lambda e, tt=tt: e.scalar_tensor_tensor(
            eqb[:, tt, :], eqb[:, tt, :], cst[:, 1:2], prb[:, tt, :],
            op0=ALU.mult, op1=ALU.add), vA)
    wait("vector", vA, last)
    t2 = inc("vector", lambda e: e.reduce_max(
        rst[:, 1, :].unsqueeze(-1), eqb[:, :, :], axis=AX.X), vA)
    wait("vector", vA, t2)
    for tt in range(NTT):
        last = inc("vector", lambda e, tt=tt: e.tensor_scalar(
            selb[:, tt, :], prb[:, tt, :], rst[:, 1, tt:tt + 1], None,
            op0=ALU.is_ge), vA)
    wait("vector", vA, last)
    t3 = inc("vector", lambda e: e.tensor_mul(
        selb[:, :, :].rearrange("p a b -> p (a b)"),
        selb[:, :, :].rearrange("p a b -> p (a b)"),
        prb[:, :, :].rearrange("p a b -> p (a b)")), vA)
    wait("vector", vA, t3)
    t4 = inc("vector", lambda e: e.reduce_sum(
        rst[:, 2, :].unsqueeze(-1), selb[:, :, :], axis=AX.X), vA)
    wait("vector", vA, t4)
    t5 = inc("vector", lambda e: e.reciprocal(rst[:, 3, :], rst[:, 2, :]), vA)
    wait("vector", vA, t5)
    for tt in range(NTT):
        last = inc("vector", lambda e, tt=tt: e.tensor_scalar(
            wrt[:, tt, :], selb[:, tt, :], rst[:, 3, tt:tt + 1], None,
            op0=ALU.mult), vA)
    wait("vector", vA, last)
    t_rout_v = vA.n
    t_rout_mm = tMM.n
    if upto == "rout":
        dump_and_stop()

    gate_srcs = [sgwT_d] + [gw_d[e] for e in range(NE)]
    up_srcs = [suwT_d] + [uw_d[e] for e in range(NE)]
    down_srcs = [sdwT_d] + [dw_d[e] for e in range(NE)]

    grp_rel = [[t_q_mm, t_q_mm], [t_q_mm, t_q_mm]]
    dwe_rel = t_rout_mm
    act_free_mm = t_rout_mm
    gsl = [0, 0]
    last_r2 = 0

    for ei in range(NE + 1):
        t_dwe = 0
        for th in range(2):
            tbase = th * 1024
            unit = 0
            act_tok = [0] * NIC
            first_act = True
            for ig in range(8):
                pp = ig % 2
                wait("gpsimd", tMM, grp_rel[pp][0])
                tg = inc("gpsimd", lambda e, ei=ei, ig=ig, pp=pp: e.dma_start(
                    out=wgrp[:, pp, 0, :, :],
                    in_=gate_srcs[ei][:, ig * 256:(ig + 1) * 256]
                    .rearrange("(c p) i -> p c i", p=128)), dWg[pp], 16)
                wait("gpsimd", tMM, grp_rel[pp][1])
                tu = inc("gpsimd", lambda e, ei=ei, ig=ig, pp=pp: e.dma_start(
                    out=wgrp[:, pp, 1, :, :],
                    in_=up_srcs[ei][:, ig * 256:(ig + 1) * 256]
                    .rearrange("(c p) i -> p c i", p=128)), dWu[pp], 16)
                if th == 0 and 2 <= ig <= 5:
                    wait("gpsimd", tMM, dwe_rel)
                    qq = ig - 2
                    t_dwe = inc("gpsimd", lambda e, ei=ei, qq=qq: e.dma_start(
                        out=dwe[:, 4 * qq:4 * (qq + 1), :],
                        in_=down_srcs[ei][512 * qq:512 * (qq + 1), :]
                        .rearrange("(c p) h -> p c h", p=128)), dWd, 16)
                wait("tensor", dWg[pp], tg)
                wait("tensor", dWu[pp], tu)
                for ic in range(2):
                    i = ig * 2 + ic
                    for ts in range(2):
                        gb = (psA, psB)[unit % 2]
                        ub = (psC, psD)[unit % 2]
                        bwait("tensor", gb)
                        bwait("tensor", ub)
                        tokg = toku = 0
                        for hc in range(NHC):
                            tokg = inc("tensor", lambda e, pp=pp, hc=hc, ic=ic,
                                       ts=ts, gb=gb, tbase=tbase: e.matmul(
                                           gb[:, :],
                                           wgrp[:, pp, 0, hc,
                                                ic * 128:(ic + 1) * 128],
                                           XQ[:, hc, tbase + ts * 512:
                                              tbase + (ts + 1) * 512],
                                           start=(hc == 0),
                                           stop=(hc == NHC - 1)), tMM)
                        for hc in range(NHC):
                            toku = inc("tensor", lambda e, pp=pp, hc=hc, ic=ic,
                                       ts=ts, ub=ub, tbase=tbase: e.matmul(
                                           ub[:, :],
                                           wgrp[:, pp, 1, hc,
                                                ic * 128:(ic + 1) * 128],
                                           XQ[:, hc, tbase + ts * 512:
                                              tbase + (ts + 1) * 512],
                                           start=(hc == 0),
                                           stop=(hc == NHC - 1)), tMM)
                        wait("scalar", tMM, tokg)
                        wait("scalar", vA, gsl[unit % 2])
                        if ei == 0 and th == 0 and unit == 0:
                            wait("scalar", sA, lnsq_prev[0])
                        sil = inc("scalar", lambda e, unit=unit, gb=gb:
                                  e.activation(gscr[:, unit % 2, :],
                                               gb[:, :], AF.Silu), sA)
                        bset(gb, sA, sil)
                        wait("vector", tMM, toku)
                        wait("vector", sA, sil)
                        if first_act:
                            wait("vector", tMM, act_free_mm)
                            first_act = False
                        mu = inc("vector", lambda e, unit=unit, i=i,
                                 ts=ts, ub=ub: e.tensor_tensor(
                                     act[:, i, ts * 512:(ts + 1) * 512],
                                     gscr[:, unit % 2, :], ub[:, :],
                                     op=ALU.mult), vA)
                        bset(ub, vA, mu)
                        gsl[unit % 2] = mu
                        act_tok[i] = mu
                        unit += 1
                grp_rel[pp][0] = tMM.n
                grp_rel[pp][1] = tMM.n
            wait("tensor", dWd, t_dwe)
            for i in range(NIC):
                wait("tensor", vA, act_tok[i])
            for tt in range(8):
                gtt = th * 8 + tt
                for hh in range(2):
                    b = (psE, psF)[(tt * 2 + hh) % 2]
                    bwait("tensor", b)
                    tok = 0
                    for i in range(NIC):
                        tok = inc("tensor", lambda e, i=i, tt=tt, hh=hh, b=b:
                                  e.matmul(b[:, :],
                                           act[:, i, tt * 128:(tt + 1) * 128],
                                           dwe[:, i, hh * 512:(hh + 1) * 512],
                                           start=(i == 0),
                                           stop=(i == NIC - 1)), tMM)
                    wait("vector", tMM, tok)
                    if ei == 0:
                        last_r2 = inc("vector", lambda e, gtt=gtt, hh=hh, b=b:
                                      e.tensor_tensor(
                                          R2[:, gtt, hh * 512:(hh + 1) * 512],
                                          R2[:, gtt, hh * 512:(hh + 1) * 512],
                                          b[:, :], op=ALU.add), vA)
                    else:
                        last_r2 = inc("vector", lambda e, gtt=gtt, hh=hh, b=b,
                                      ei=ei: e.scalar_tensor_tensor(
                                          R2[:, gtt, hh * 512:(hh + 1) * 512],
                                          b[:, :], wrt[:, gtt, ei - 1:ei],
                                          R2[:, gtt, hh * 512:(hh + 1) * 512],
                                          op0=ALU.mult, op1=ALU.add), vA)
                    bset(b, vA, last_r2)
                    if ei == NE and hh == 1:
                        wait("sync", vA, last_r2)
                        inc("sync", lambda e, gtt=gtt: e.dma_start(
                            out=out_d[gtt * 128:(gtt + 1) * 128, :],
                            in_=R2[:, gtt, :]), dO, 16)
            act_free_mm = tMM.n
        dwe_rel = tMM.n
        if upto == ("ffn%d" % ei):
            dump_and_stop()

    if not dbg["stopped"]:
        wait("sync", dO, 16 * NTT)

    with nc.Block() as block:
        @block.sync
        def _(eng):
            for f in prog["sync"]:
                f(eng)

        @block.gpsimd
        def _(eng):
            for f in prog["gpsimd"]:
                f(eng)

        @block.tensor
        def _(eng):
            for f in prog["tensor"]:
                f(eng)

        @block.vector
        def _(eng):
            for f in prog["vector"]:
                f(eng)

        @block.scalar
        def _(eng):
            for f in prog["scalar"]:
                f(eng)

    return nc


def kernel(L0_attn_x, L0_attn_Wq, L0_attn_Wk, L0_attn_Wv, L0_ffn_shared_gate,
           L0_ffn_shared_up, L0_ffn_shared_down, L0_ffn_gate_w, L0_ffn_up_w,
           L0_ffn_down_w, L0_ffn_router_w, L0_ffn_routing_bias):
    x = np.ascontiguousarray(np.asarray(L0_attn_x, dtype=np.float32))
    wq = np.ascontiguousarray(np.asarray(L0_attn_Wq, dtype=np.float32))
    wk = np.ascontiguousarray(np.asarray(L0_attn_Wk, dtype=np.float32))
    wv = np.ascontiguousarray(np.asarray(L0_attn_Wv, dtype=np.float32))
    sgwT = np.ascontiguousarray(np.asarray(L0_ffn_shared_gate, dtype=np.float32).T)
    suwT = np.ascontiguousarray(np.asarray(L0_ffn_shared_up, dtype=np.float32).T)
    sdwT = np.ascontiguousarray(np.asarray(L0_ffn_shared_down, dtype=np.float32).T)
    gw = np.ascontiguousarray(np.asarray(L0_ffn_gate_w, dtype=np.float32))
    uw = np.ascontiguousarray(np.asarray(L0_ffn_up_w, dtype=np.float32))
    dw = np.ascontiguousarray(np.asarray(L0_ffn_down_w, dtype=np.float32))
    rwT = np.ascontiguousarray(np.asarray(L0_ffn_router_w, dtype=np.float32).T)
    rbias = np.ascontiguousarray(np.broadcast_to(
        np.asarray(L0_ffn_routing_bias, dtype=np.float32), (128, NE)))
    ident = np.eye(128, dtype=np.float32)
    pp = np.arange(128)
    dmask = np.where(pp[None, :] <= pp[:, None], 0.0, NEG_BIG).astype(np.float32)

    nc = build_program()
    in_maps = []
    for c in range(8):
        in_maps.append({
            "x": np.ascontiguousarray(x[c]), "wq": wq, "wk": wk, "wv": wv,
            "sgwT": sgwT, "suwT": suwT, "sdwT": sdwT, "gw": gw, "uw": uw,
            "dw": dw, "rwT": rwT, "rbias": rbias, "ident": ident,
            "dmask": dmask,
        })
    res = run_bass_kernel_spmd(nc, in_maps, core_ids=list(range(8)))
    out = np.stack([res.results[c]["out"] for c in range(8)], axis=0)
    return out.astype(np.float32)


# version 26
# speedup vs baseline: 1.0930x; 1.0930x over previous
import numpy as np
import concourse.bass as bass
import concourse.mybir as mybir
from concourse.bass_utils import run_bass_kernel_spmd
from concourse.hw_specs import get_activation_tables

F32 = mybir.dt.float32
BF16 = mybir.dt.bfloat16
AF = mybir.ActivationFunctionType
ALU = mybir.AluOpType
AX = mybir.AxisListType

S = 2048
D = 1024
I = 2048
NE = 8
NTT = S // 128
NHC = D // 128
NIC = I // 128
NEG_BIG = -30000.0


class Ctr:
    def __init__(self, nc, name):
        self.h = nc.alloc_semaphore(name)
        self.n = 0


def build_program(upto=None):
    nc = bass.Bass("TRN2", target_bir_lowering=False)

    x_d = nc.dram_tensor("x", [S, D], F32, kind="ExternalInput").ap()
    wq_d = nc.dram_tensor("wq", [D, D], F32, kind="ExternalInput").ap()
    wk_d = nc.dram_tensor("wk", [D, D], F32, kind="ExternalInput").ap()
    wv_d = nc.dram_tensor("wv", [D, D], F32, kind="ExternalInput").ap()
    sgwT_d = nc.dram_tensor("sgwT", [D, I], F32, kind="ExternalInput").ap()
    suwT_d = nc.dram_tensor("suwT", [D, I], F32, kind="ExternalInput").ap()
    sdwT_d = nc.dram_tensor("sdwT", [I, D], F32, kind="ExternalInput").ap()
    gw_d = nc.dram_tensor("gw", [NE, D, I], F32, kind="ExternalInput").ap()
    uw_d = nc.dram_tensor("uw", [NE, D, I], F32, kind="ExternalInput").ap()
    dw_d = nc.dram_tensor("dw", [NE, I, D], F32, kind="ExternalInput").ap()
    rwT_d = nc.dram_tensor("rwT", [D, NE], F32, kind="ExternalInput").ap()
    rbias_d = nc.dram_tensor("rbias", [128, NE], F32, kind="ExternalInput").ap()
    ident_d = nc.dram_tensor("ident", [128, 128], F32, kind="ExternalInput").ap()
    dmask_d = nc.dram_tensor("dmask", [128, 128], F32, kind="ExternalInput").ap()
    out_d = nc.dram_tensor("out", [S, D], F32, kind="ExternalOutput").ap()

    R2 = nc.alloc_sbuf_tensor("R2", [128, NTT, D], F32)
    XQ = nc.alloc_sbuf_tensor("XQ", [128, NHC, S], BF16)
    KBUF = nc.alloc_sbuf_tensor("KBUF", [128, 16384], BF16)
    VBUF = nc.alloc_sbuf_tensor("VBUF", [128, 16384], BF16)
    WBUF = nc.alloc_sbuf_tensor("WBUF", [128, 8192], BF16)
    sS = nc.alloc_sbuf_tensor("sS", [128, 5760], BF16)
    sP = nc.alloc_sbuf_tensor("sP", [128, 2, S], BF16)
    sPT = nc.alloc_sbuf_tensor("sPT", [128, 8, 128], BF16)
    xlntm = nc.alloc_sbuf_tensor("xlntm", [128, 2, D], BF16)
    gscr = nc.alloc_sbuf_tensor("gscr", [128, 2, 512], BF16)
    ident = nc.alloc_sbuf_tensor("identsb", [128, 128], BF16)
    dmask = nc.alloc_sbuf_tensor("dmasksb", [128, 128], BF16)
    rwTs = nc.alloc_sbuf_tensor("rwTs", [128, NHC, NE], BF16)
    rbias = nc.alloc_sbuf_tensor("rbiassb", [128, NE], F32)
    lgt = nc.alloc_sbuf_tensor("lgt", [128, NTT, NE], F32)
    prb = nc.alloc_sbuf_tensor("prb", [128, NTT, NE], F32)
    eqb = nc.alloc_sbuf_tensor("eqb", [128, NTT, NE], F32)
    selb = nc.alloc_sbuf_tensor("selb", [128, NTT, NE], F32)
    rst = nc.alloc_sbuf_tensor("rst", [128, 4, NTT], F32)
    wrt = nc.alloc_sbuf_tensor("wrt", [128, NTT, NE], F32)
    qst = nc.alloc_sbuf_tensor("qst", [128, 4, NTT], F32)
    lst = nc.alloc_sbuf_tensor("lst", [128, 5, NTT], F32)
    cst = nc.alloc_sbuf_tensor("cst", [128, 4], F32)
    qcm = nc.alloc_sbuf_tensor("qcm", [128, 3, 4], F32)

    _sS_off = (0, 2048, 3840)

    def sSs(qi, a, b):
        o = _sS_off[qi % 3]
        return sS[:, o + a:o + b]

    kfm = KBUF[:, :].rearrange("p (c t) -> p c t", c=NHC)
    act = KBUF[:, :].rearrange("p (c t) -> p c t", c=NIC)
    vtm = VBUF[:, :].rearrange("p (c t) -> p c t", c=NTT)
    dwe = VBUF[:, :].rearrange("p (c t) -> p c t", c=NIC)
    wqkv = WBUF[:, :].rearrange("p (c t) -> p c t", c=NHC)
    wgrp = WBUF[:, :].rearrange("p (b g c t) -> p b g c t", b=2, g=2, c=NHC)

    psA = nc.alloc_psum_tensor("psA", [128, 512], F32)
    psB = nc.alloc_psum_tensor("psB", [128, 512], F32)
    psC = nc.alloc_psum_tensor("psC", [128, 512], F32)
    psD = nc.alloc_psum_tensor("psD", [128, 512], F32)
    psE = nc.alloc_psum_tensor("psE", [128, 512], F32)
    psF = nc.alloc_psum_tensor("psF", [128, 512], F32)
    ptA = nc.alloc_psum_tensor("ptA", [128, 256], BF16)
    ptB = nc.alloc_psum_tensor("ptB", [128, 256], BF16)

    ptb = [ptA[:, :], ptB[:, :],
           psE[:, :].bitcast(BF16)[:, 0:256], psF[:, :].bitcast(BF16)[:, 0:256]]

    dWc = Ctr(nc, "dWc")
    dWw = Ctr(nc, "dWw")
    dWp = Ctr(nc, "dWp")
    dWg = [Ctr(nc, "dWg0"), Ctr(nc, "dWg1")]
    dWu = [Ctr(nc, "dWu0"), Ctr(nc, "dWu1")]
    dWd = Ctr(nc, "dWd")
    dXc = [Ctr(nc, "dXc%d" % j) for j in range(4)]
    dO = Ctr(nc, "dO")
    tMM = Ctr(nc, "tMM")
    tTR = Ctr(nc, "tTR")
    vA = Ctr(nc, "vA")
    sA = Ctr(nc, "sA")

    prog = {k: [] for k in ("sync", "gpsimd", "tensor", "vector", "scalar")}
    dbg = {"stopped": False}

    def em(engine, fn):
        if not dbg["stopped"]:
            prog[engine].append(fn)

    def wait(engine, ctr, tok):
        if tok > 0:
            em(engine, lambda eng, c=ctr, t=tok: eng.wait_ge(c.h, t))

    def inc(engine, build, ctr, k=1):
        ctr.n += k
        tok = ctr.n
        em(engine, lambda eng, b=build, c=ctr, kk=k: b(eng).then_inc(c.h, kk))
        return tok

    rel = {}

    def bwait(engine, bank):
        r = rel.get(id(bank))
        if r is not None:
            wait(engine, r[0], r[1])

    def bset(bank, ctr, tok):
        rel[id(bank)] = (ctr, tok)

    def dump_and_stop():
        wait("sync", vA, vA.n)
        wait("sync", sA, sA.n)
        wait("sync", tMM, tMM.n)
        wait("sync", tTR, tTR.n)
        n = 0
        for name, buf, shape, dt in (
                ("d_r2", R2, [128, NTT * D], F32),
                ("d_xq", XQ, [128, NHC * S], BF16),
                ("d_k", KBUF, [128, 16384], BF16),
                ("d_v", VBUF, [128, 16384], BF16),
                ("d_qst", qst, [128, 4 * NTT], F32),
                ("d_lst", lst, [128, 5 * NTT], F32),
                ("d_wrt", wrt, [128, NTT * NE], F32),
                ("d_prb", prb, [128, NTT * NE], F32),
        ):
            dr = nc.dram_tensor(name, shape, dt, kind="ExternalOutput").ap()
            ap = buf[:, :] if len(buf.shape) == 2 else buf[:, :, :].rearrange(
                "p a b -> p (a b)")
            inc("sync", lambda e, dr=dr, ap=ap: e.dma_start(out=dr, in_=ap),
                dO, 16)
            n += 1
        wait("sync", dO, 16 * n)
        dbg["stopped"] = True

    t_xc = []
    for j in range(4):
        t_xc.append((dXc[j], inc("sync", lambda e, j=j: e.dma_start(
            out=R2[:, 4 * j:4 * (j + 1), :],
            in_=x_d[512 * j:512 * (j + 1), :]
            .rearrange("(c p) h -> p c h", p=128)), dXc[j], 16)))
    inc("gpsimd", lambda e: e.dma_start(out=dmask[:, :], in_=dmask_d), dWc, 16)
    inc("gpsimd", lambda e: e.dma_start(out=ident[:, :], in_=ident_d), dWc, 16)
    inc("gpsimd", lambda e: e.dma_start(out=rbias[:, :], in_=rbias_d), dWc, 16)
    t_rwT = inc("gpsimd", lambda e: e.dma_start(
        out=rwTs[:, :, :], in_=rwT_d.rearrange("(c p) e -> p c e", p=128)), dWc, 16)
    t_dmask = t_ident = t_rbias = t_rwT
    t_wk = inc("gpsimd", lambda e: e.dma_start(
        out=wqkv[:, :, :], in_=wk_d.rearrange("(c p) e -> p c e", p=128)), dWw, 16)
    inc("gpsimd", lambda e: e.dma_start(
        out=sS[:, 0:4096].rearrange("p (c t) -> p c t", c=4),
        in_=wv_d[0:512, :].rearrange("(c p) e -> p c e", p=128)), dWp, 16)
    t_wv_park = inc("gpsimd", lambda e: e.dma_start(
        out=sP[:, :, :].rearrange("p a b -> p (a b)").rearrange(
            "p (c t) -> p c t", c=4),
        in_=wv_d[512:1024, :].rearrange("(c p) e -> p c e", p=128)), dWp, 16)

    em("vector", lambda e: e.memset(cst[:, 0:1], 1e-5))
    em("vector", lambda e: e.memset(cst[:, 1:2], -2.0))


    lnsq = [0] * NTT
    lnsq_prev = [0]
    lntr = [0] * NTT

    def ln_stats(tt, v_pre=0, v_dx=None):
        if v_dx is not None:
            wait("vector", v_dx[0], v_dx[1])
        wait("vector", vA, v_pre)
        r_tok = inc("vector", lambda e, tt=tt: e.reduce_sum(
            lst[:, 0, tt:tt + 1], R2[:, tt, :], axis=AX.X, negate=True), vA)
        wait("vector", vA, r_tok)
        m_tok = inc("vector", lambda e, tt=tt: e.tensor_scalar(
            lst[:, 1, tt:tt + 1], lst[:, 0, tt:tt + 1], 1.0 / D, None,
            op0=ALU.mult), vA)
        wait("scalar", vA, m_tok)
        if v_dx is not None:
            wait("scalar", v_dx[0], v_dx[1])
        wait("scalar", sA, lnsq_prev[0])
        sq_tok = inc("scalar", lambda e, tt=tt: e.activation(
            gscr[:, :, :].rearrange("p a b -> p (a b)"), R2[:, tt, :],
            AF.Square, bias=lst[:, 1, tt:tt + 1],
            scale=1.0, accum_out=lst[:, 2, tt:tt + 1]), sA)
        lnsq_prev[0] = sq_tok
        lnsq[tt] = sq_tok
        return sq_tok

    lnap = [0] * NTT

    def ln_fin_a(tt):
        wait("scalar", sA, lnsq[tt])
        l1 = inc("scalar", lambda e, tt=tt: e.activation(
            lst[:, 3, tt:tt + 1], lst[:, 2, tt:tt + 1], AF.Ln,
            bias=cst[:, 0:1], scale=1.0 / D), sA)
        wait("scalar", sA, l1)
        l2 = inc("scalar", lambda e, tt=tt: e.activation(
            lst[:, 4, tt:tt + 1], lst[:, 3, tt:tt + 1], AF.Exp,
            scale=-0.5), sA)
        wait("vector", sA, l2)
        if tt >= 2:
            wait("vector", tTR, lntr[tt - 2])
        lnap[tt] = inc("vector", lambda e, tt=tt: e.tensor_scalar(
            xlntm[:, tt % 2, :], R2[:, tt, :], lst[:, 1, tt:tt + 1],
            lst[:, 4, tt:tt + 1], op0=ALU.add, op1=ALU.mult), vA)

    def ln_fin_b(tt):
        wait("tensor", vA, lnap[tt])
        wait("tensor", dWc, t_ident)
        xq_tok = 0
        for hp in range(NHC // 2):
            pt = ptb[hp % 4]
            bwait("tensor", pt)
            tr_tok = 0
            for j in range(2):
                hc = 2 * hp + j
                tr_tok = inc("tensor", lambda e, tt=tt, hc=hc, j=j, pt=pt:
                             e.transpose(
                                 pt[:, j * 128:(j + 1) * 128],
                                 xlntm[:, tt % 2, hc * 128:(hc + 1) * 128],
                                 ident[:, :]), tTR)
            lntr[tt] = tr_tok
            wait("scalar", tTR, tr_tok)
            xq_tok = inc("scalar", lambda e, tt=tt, hp=hp, pt=pt: e.copy(
                XQ[:, 2 * hp:2 * hp + 2, tt * 128:(tt + 1) * 128],
                pt[:, :].rearrange("p (a b) -> p a b", a=2)), sA)
            bset(pt, sA, xq_tok)
        return xq_tok

    def ln_finish():
        xq_tok = 0
        for tt in range(NTT):
            ln_fin_a(tt)
            xq_tok = ln_fin_b(tt)
        return xq_tok

    for tt in range(NTT):
        ln_stats(tt, v_pre=0, v_dx=t_xc[tt // 4])
    qkv_banks = (psA, psB, psC, psD)
    bi = 0
    wait("tensor", dWw, t_wk)
    xq_ready_s = 0
    for ts in range(4):
        for tt4 in range(4):
            tt = 4 * ts + tt4
            ln_fin_a(tt)
            xq_ready_s = ln_fin_b(tt)
        wait("tensor", sA, xq_ready_s)
        for ec in range(NHC):
            b = qkv_banks[bi % 4]
            bwait("tensor", b)
            tok = 0
            for d in range(NHC):
                tok = inc("tensor", lambda e, d=d, ec=ec, ts=ts, b=b: e.matmul(
                    b[:, :], wqkv[:, d, ec * 128:(ec + 1) * 128],
                    XQ[:, d, ts * 512:(ts + 1) * 512],
                    start=(d == 0), stop=(d == NHC - 1)), tMM)
            wait("scalar", tMM, tok)
            ev = inc("scalar", lambda e, ec=ec, ts=ts, b=b: e.copy(
                kfm[:, ec, ts * 512:(ts + 1) * 512], b[:, :]), sA)
            bset(b, sA, ev)
            bi += 1
    t_k_mm = tMM.n

    wait("gpsimd", tMM, t_k_mm)
    t_wq = inc("gpsimd", lambda e: e.dma_start(
        out=wqkv[:, :, :], in_=wq_d.rearrange("(c p) e -> p c e", p=128)), dWw, 16)

    wait("tensor", dWp, t_wv_park)
    wv_lo = sS[:, 0:4096].rearrange("p (c t) -> p c t", c=4)
    wv_hi = sP[:, :, :].rearrange("p a b -> p (a b)").rearrange(
        "p (c t) -> p c t", c=4)
    for tt in range(NTT):
        for eh in range(2):
            b = qkv_banks[bi % 4]
            bwait("tensor", b)
            tok = 0
            for d in range(NHC):
                wvsrc = wv_lo if d < 4 else wv_hi
                tok = inc("tensor", lambda e, d=d, tt=tt, eh=eh, b=b,
                          wvsrc=wvsrc: e.matmul(
                    b[:, :], XQ[:, d, tt * 128:(tt + 1) * 128],
                    wvsrc[:, d % 4, eh * 512:(eh + 1) * 512],
                    start=(d == 0), stop=(d == NHC - 1)), tMM)
            wait("scalar", tMM, tok)
            ev = inc("scalar", lambda e, tt=tt, eh=eh, b=b: e.copy(
                vtm[:, tt, eh * 512:(eh + 1) * 512], b[:, :]), sA)
            bset(b, sA, ev)
            bi += 1
    t_v_mm = tMM.n

    wait("tensor", dWw, t_wq)
    qsg = sS[:, 0:4096].rearrange(
        "p (s c t) -> p s c t", s=2, c=NHC)
    xqcopy_tok = [0] * 8
    ev_last = [0] * 8
    for t8 in range(8):
        if t8 >= 2:
            wait("scalar", sA, xqcopy_tok[t8 - 2])
        for b in qkv_banks:
            bwait("tensor", b)
        for bk in range(4):
            b = qkv_banks[bk]
            tok = 0
            for eci in range(2):
                ec = 2 * bk + eci
                half = eci * 256
                for d in range(NHC):
                    tok = inc("tensor", lambda e, d=d, ec=ec, t8=t8, b=b,
                              half=half: e.matmul(
                                  b[:, half:half + 256],
                                  wqkv[:, d, ec * 128:(ec + 1) * 128],
                                  XQ[:, d, t8 * 256:(t8 + 1) * 256],
                                  start=(d == 0), stop=(d == NHC - 1)), tMM)
            wait("scalar", tMM, tok)
            ev = inc("scalar", lambda e, bk=bk, t8=t8, b=b: e.copy(
                qsg[:, t8 % 2, 2 * bk:2 * bk + 2, :],
                b[:, :].rearrange("p (a b) -> p a b", a=2)), sA)
            bset(b, sA, ev)
            ev_last[t8] = ev
        if t8 > 0:
            wait("scalar", sA, ev_last[t8 - 1])
            xqcopy_tok[t8 - 1] = inc("scalar", lambda e, t8=t8: e.copy(
                XQ[:, :, (t8 - 1) * 256:t8 * 256],
                qsg[:, (t8 - 1) % 2, :, :]), sA)
    wait("scalar", sA, ev_last[7])
    t_q_xq = inc("scalar", lambda e: e.copy(
        XQ[:, :, 7 * 256:8 * 256], qsg[:, 7 % 2, :, :]), sA)
    xqcopy_tok[7] = t_q_xq
    t_q_mm = tMM.n
    if upto == "qkv":
        dump_and_stop()

    wait("tensor", dWc, t_dmask)
    wait("tensor", sA, t_q_xq)
    wait("vector", dWc, t_dmask)

    exp_tok = [0] * NTT
    trs_tok = [0] * NTT
    evs_tok = [0] * NTT
    evv_tok = [0] * NTT
    sc_bi = [0]

    def emit_scores(qi):
        nk = qi + 1
        ncol = nk * 128
        nsteps = (ncol + 511) // 512
        for ks in range(nsteps):
            w = min(512, ncol - ks * 512)
            b = (psA, psB)[sc_bi[0] % 2]
            sc_bi[0] += 1
            bwait("tensor", b)
            tok = 0
            for ec in range(NHC):
                tok = inc("tensor", lambda e, ec=ec, qi=qi, ks=ks, w=w, b=b:
                          e.matmul(b[:, 0:w],
                                   XQ[:, ec, qi * 128:(qi + 1) * 128],
                                   kfm[:, ec, ks * 512:ks * 512 + w],
                                   start=(ec == 0), stop=(ec == NHC - 1)), tMM)
            if ks < nsteps - 1:
                wait("scalar", tMM, tok)
                if qi >= 3 and ks == 0:
                    wait("scalar", sA, exp_tok[qi - 3])
                evs_tok[qi] = inc("scalar", lambda e, qi=qi, ks=ks, w=w, b=b:
                                  e.copy(sSs(qi, ks * 512, ks * 512 + w),
                                         b[:, 0:w]), sA)
                bset(b, sA, evs_tok[qi])
                wait("vector", sA, evs_tok[qi])
                inc("vector", lambda e, qi=qi, ks=ks, w=w: e.reduce_max(
                    qcm[:, qi % 3, ks:ks + 1],
                    sSs(qi, ks * 512, ks * 512 + w), axis=AX.X), vA)
            else:
                wait("vector", tMM, tok)
                if qi >= 3:
                    wait("vector", sA, exp_tok[qi - 3])
                if w > 128:
                    inc("vector", lambda e, qi=qi, ks=ks, w=w, b=b:
                        e.tensor_copy(sSs(qi, ks * 512, ks * 512 + w - 128),
                                      b[:, 0:w - 128]), vA)
                evv_tok[qi] = inc("vector", lambda e, qi=qi, ks=ks, w=w, b=b:
                                  e.tensor_tensor(
                                      sSs(qi, qi * 128, (qi + 1) * 128),
                                      b[:, w - 128:w], dmask[:, :],
                                      op=ALU.add), vA)
                bset(b, vA, evv_tok[qi])
                wait("vector", vA, evv_tok[qi])
                evv_tok[qi] = inc("vector", lambda e, qi=qi, ks=ks, w=w:
                                  e.reduce_max(
                                      qcm[:, qi % 3, ks:ks + 1],
                                      sSs(qi, ks * 512, ks * 512 + w),
                                      axis=AX.X), vA)

    def emit_softmax(qi):
        ncol = (qi + 1) * 128
        nsteps = (ncol + 511) // 512
        wait("vector", vA, evv_tok[qi])
        m_tok = inc("vector", lambda e, qi=qi, nsteps=nsteps: e.reduce_max(
            qst[:, 0, qi:qi + 1], qcm[:, qi % 3, 0:nsteps], axis=AX.X), vA)
        wait("vector", vA, m_tok)
        negb_tok = inc("vector", lambda e, qi=qi: e.tensor_scalar(
            qst[:, 1, qi:qi + 1], qst[:, 0, qi:qi + 1], -1.0 / 32.0, None,
            op0=ALU.mult), vA)
        wait("scalar", vA, negb_tok)
        if qi >= 2:
            wait("scalar", tTR, trs_tok[qi - 2])
        exp_tok[qi] = inc("scalar", lambda e, qi=qi, ncol=ncol: e.activation(
            sP[:, qi % 2, 0:ncol], sSs(qi, 0, ncol), AF.Exp,
            bias=qst[:, 1, qi:qi + 1], scale=1.0 / 32.0,
            accum_out=qst[:, 2, qi:qi + 1]), sA)

    def emit_ptpv(qi):
        nk = qi + 1
        npair = (nk + 1) // 2
        wait("tensor", sA, exp_tok[qi])
        bwait("tensor", psC)
        bwait("tensor", psD)
        pte = {}
        tr = []
        tok_pv = 0

        def do_pair(p):
            pt = ptb[p % 4]
            bwait("tensor", pt)
            cnt = 1 if (p == npair - 1 and nk % 2 == 1) else 2
            tr_tok = 0
            for j in range(cnt):
                kn = 2 * p + j
                tr_tok = inc("tensor", lambda e, qi=qi, kn=kn, j=j, pt=pt:
                             e.transpose(pt[:, j * 128:(j + 1) * 128],
                                         sP[:, qi % 2, kn * 128:(kn + 1) * 128],
                                         ident[:, :]), tTR)
            tr.append(tr_tok)
            wait("scalar", tTR, tr_tok)
            s = (2 * p) % 8
            pte[p] = inc("scalar", lambda e, p=p, pt=pt, s=s, cnt=cnt: e.copy(
                sPT[:, s:s + cnt, :].rearrange("p a b -> p (a b)"),
                pt[:, 0:cnt * 128]), sA)
            bset(pt, sA, pte[p])

        do_pair(0)
        if npair > 1:
            do_pair(1)
        for p in range(npair):
            if p + 2 < npair:
                do_pair(p + 2)
            wait("tensor", sA, pte[p])
            cnt = 1 if (p == npair - 1 and nk % 2 == 1) else 2
            for j in range(cnt):
                kc = 2 * p + j
                for eh, bb in ((0, psC), (1, psD)):
                    tok_pv = inc("tensor", lambda e, kc=kc, eh=eh, bb=bb, nk=nk:
                                 e.matmul(bb[:, :], sPT[:, kc % 8, :],
                                          vtm[:, kc, eh * 512:(eh + 1) * 512],
                                          start=(kc == 0),
                                          stop=(kc == nk - 1)), tMM)
        trs_tok[qi] = tr[-1]
        return tok_pv

    def emit_epilogue(qi, tok_pv):
        wait("vector", sA, exp_tok[qi])
        linv_tok = inc("vector", lambda e, qi=qi: e.reciprocal(
            qst[:, 3, qi:qi + 1], qst[:, 2, qi:qi + 1]), vA)
        wait("vector", vA, linv_tok)
        wait("vector", tMM, tok_pv)
        pv = 0
        for eh, bb in ((0, psC), (1, psD)):
            pv = inc("vector", lambda e, qi=qi, eh=eh, bb=bb:
                     e.scalar_tensor_tensor(
                         R2[:, qi, eh * 512:(eh + 1) * 512], bb[:, :],
                         qst[:, 3, qi:qi + 1],
                         R2[:, qi, eh * 512:(eh + 1) * 512],
                         op0=ALU.mult, op1=ALU.add), vA)
            bset(bb, vA, pv)
        return pv

    emit_scores(0)
    emit_scores(1)
    emit_softmax(0)
    for qi in range(NTT):
        if qi + 1 < NTT:
            emit_softmax(qi + 1)
        if qi + 2 < NTT:
            emit_scores(qi + 2)
        tok_pv = emit_ptpv(qi)
        pv = emit_epilogue(qi, tok_pv)
        ln_stats(qi, v_pre=pv)
        if qi >= 1:
            ln_fin_a(qi - 1)
        if qi >= 2:
            ln_fin_b(qi - 2)
    t_attn_v = vA.n
    if upto == "attn":
        dump_and_stop()

    ln_fin_a(NTT - 1)
    xq2_ready_s = ln_fin_b(NTT - 2)
    xq2_ready_s = ln_fin_b(NTT - 1)
    if upto == "ln2":
        dump_and_stop()

    wait("tensor", sA, xq2_ready_s)
    wait("tensor", dWc, t_rwT)
    bwait("tensor", psE)
    logit_prev = 0
    for tt in range(NTT):
        if logit_prev:
            wait("tensor", vA, logit_prev)
        tok = 0
        for hc in range(NHC):
            tok = inc("tensor", lambda e, tt=tt, hc=hc: e.matmul(
                psE[:, 0:NE], XQ[:, hc, tt * 128:(tt + 1) * 128], rwTs[:, hc, :],
                start=(hc == 0), stop=(hc == NHC - 1)), tMM)
        wait("vector", tMM, tok)
        if tt == 0:
            wait("vector", dWc, t_rbias)
        logit_prev = inc("vector", lambda e, tt=tt: e.tensor_tensor(
            lgt[:, tt, :], psE[:, 0:NE], rbias[:, :], op=ALU.add), vA)
    bset(psE, vA, logit_prev)
    wait("scalar", vA, logit_prev)
    sg_tok = inc("scalar", lambda e: e.activation(
        prb[:, :, :].rearrange("p a b -> p (a b)"),
        lgt[:, :, :].rearrange("p a b -> p (a b)"), AF.Sigmoid), sA)
    wait("vector", sA, sg_tok)
    t1 = inc("vector", lambda e: e.reduce_max(
        rst[:, 0, :].unsqueeze(-1), prb[:, :, :], axis=AX.X), vA)
    wait("vector", vA, t1)
    last = 0
    for tt in range(NTT):
        last = inc("vector", lambda e, tt=tt: e.tensor_scalar(
            eqb[:, tt, :], prb[:, tt, :], rst[:, 0, tt:tt + 1], None,
            op0=ALU.is_ge), vA)
    wait("vector", vA, last)
    for tt in range(NTT):
        last = inc("vector", lambda e, tt=tt: e.scalar_tensor_tensor(
            eqb[:, tt, :], eqb[:, tt, :], cst[:, 1:2], prb[:, tt, :],
            op0=ALU.mult, op1=ALU.add), vA)
    wait("vector", vA, last)
    t2 = inc("vector", lambda e: e.reduce_max(
        rst[:, 1, :].unsqueeze(-1), eqb[:, :, :], axis=AX.X), vA)
    wait("vector", vA, t2)
    for tt in range(NTT):
        last = inc("vector", lambda e, tt=tt: e.tensor_scalar(
            selb[:, tt, :], prb[:, tt, :], rst[:, 1, tt:tt + 1], None,
            op0=ALU.is_ge), vA)
    wait("vector", vA, last)
    t3 = inc("vector", lambda e: e.tensor_mul(
        selb[:, :, :].rearrange("p a b -> p (a b)"),
        selb[:, :, :].rearrange("p a b -> p (a b)"),
        prb[:, :, :].rearrange("p a b -> p (a b)")), vA)
    wait("vector", vA, t3)
    t4 = inc("vector", lambda e: e.reduce_sum(
        rst[:, 2, :].unsqueeze(-1), selb[:, :, :], axis=AX.X), vA)
    wait("vector", vA, t4)
    t5 = inc("vector", lambda e: e.reciprocal(rst[:, 3, :], rst[:, 2, :]), vA)
    wait("vector", vA, t5)
    for tt in range(NTT):
        last = inc("vector", lambda e, tt=tt: e.tensor_scalar(
            wrt[:, tt, :], selb[:, tt, :], rst[:, 3, tt:tt + 1], None,
            op0=ALU.mult), vA)
    wait("vector", vA, last)
    t_rout_v = vA.n
    t_rout_mm = tMM.n
    if upto == "rout":
        dump_and_stop()

    gate_srcs = [sgwT_d] + [gw_d[e] for e in range(NE)]
    up_srcs = [suwT_d] + [uw_d[e] for e in range(NE)]
    down_srcs = [sdwT_d] + [dw_d[e] for e in range(NE)]

    grp_rel = [[t_q_mm, t_q_mm], [t_q_mm, t_q_mm]]
    dwe_rel = t_rout_mm
    act_free_mm = t_rout_mm
    gsl = [0, 0]
    last_r2 = 0

    for ei in range(NE + 1):
        t_dwe = 0
        for th in range(2):
            tbase = th * 1024
            unit = 0
            act_tok = [0] * NIC
            first_act = True
            for ig in range(8):
                pp = ig % 2
                wait("gpsimd", tMM, grp_rel[pp][0])
                tg = inc("gpsimd", lambda e, ei=ei, ig=ig, pp=pp: e.dma_start(
                    out=wgrp[:, pp, 0, :, :],
                    in_=gate_srcs[ei][:, ig * 256:(ig + 1) * 256]
                    .rearrange("(c p) i -> p c i", p=128)), dWg[pp], 16)
                wait("gpsimd", tMM, grp_rel[pp][1])
                tu = inc("gpsimd", lambda e, ei=ei, ig=ig, pp=pp: e.dma_start(
                    out=wgrp[:, pp, 1, :, :],
                    in_=up_srcs[ei][:, ig * 256:(ig + 1) * 256]
                    .rearrange("(c p) i -> p c i", p=128)), dWu[pp], 16)
                if th == 0 and 2 <= ig <= 5:
                    wait("gpsimd", tMM, dwe_rel)
                    qq = ig - 2
                    t_dwe = inc("gpsimd", lambda e, ei=ei, qq=qq: e.dma_start(
                        out=dwe[:, 4 * qq:4 * (qq + 1), :],
                        in_=down_srcs[ei][512 * qq:512 * (qq + 1), :]
                        .rearrange("(c p) h -> p c h", p=128)), dWd, 16)
                wait("tensor", dWg[pp], tg)
                wait("tensor", dWu[pp], tu)
                for ic in range(2):
                    i = ig * 2 + ic
                    for ts in range(2):
                        gb = (psA, psB)[unit % 2]
                        ub = (psC, psD)[unit % 2]
                        bwait("tensor", gb)
                        bwait("tensor", ub)
                        tokg = toku = 0
                        for hc in range(NHC):
                            tokg = inc("tensor", lambda e, pp=pp, hc=hc, ic=ic,
                                       ts=ts, gb=gb, tbase=tbase: e.matmul(
                                           gb[:, :],
                                           wgrp[:, pp, 0, hc,
                                                ic * 128:(ic + 1) * 128],
                                           XQ[:, hc, tbase + ts * 512:
                                              tbase + (ts + 1) * 512],
                                           start=(hc == 0),
                                           stop=(hc == NHC - 1)), tMM)
                        for hc in range(NHC):
                            toku = inc("tensor", lambda e, pp=pp, hc=hc, ic=ic,
                                       ts=ts, ub=ub, tbase=tbase: e.matmul(
                                           ub[:, :],
                                           wgrp[:, pp, 1, hc,
                                                ic * 128:(ic + 1) * 128],
                                           XQ[:, hc, tbase + ts * 512:
                                              tbase + (ts + 1) * 512],
                                           start=(hc == 0),
                                           stop=(hc == NHC - 1)), tMM)
                        wait("scalar", tMM, tokg)
                        wait("scalar", vA, gsl[unit % 2])
                        if ei == 0 and th == 0 and unit == 0:
                            wait("scalar", sA, lnsq_prev[0])
                        sil = inc("scalar", lambda e, unit=unit, gb=gb:
                                  e.activation(gscr[:, unit % 2, :],
                                               gb[:, :], AF.Silu), sA)
                        bset(gb, sA, sil)
                        wait("vector", tMM, toku)
                        wait("vector", sA, sil)
                        if first_act:
                            wait("vector", tMM, act_free_mm)
                            first_act = False
                        mu = inc("vector", lambda e, unit=unit, i=i,
                                 ts=ts, ub=ub: e.tensor_tensor(
                                     act[:, i, ts * 512:(ts + 1) * 512],
                                     gscr[:, unit % 2, :], ub[:, :],
                                     op=ALU.mult), vA)
                        bset(ub, vA, mu)
                        gsl[unit % 2] = mu
                        act_tok[i] = mu
                        unit += 1
                grp_rel[pp][0] = tMM.n
                grp_rel[pp][1] = tMM.n
            wait("tensor", dWd, t_dwe)
            for i in range(NIC):
                wait("tensor", vA, act_tok[i])
            for tt in range(8):
                gtt = th * 8 + tt
                for hh in range(2):
                    b = (psE, psF)[(tt * 2 + hh) % 2]
                    bwait("tensor", b)
                    tok = 0
                    for i in range(NIC):
                        tok = inc("tensor", lambda e, i=i, tt=tt, hh=hh, b=b:
                                  e.matmul(b[:, :],
                                           act[:, i, tt * 128:(tt + 1) * 128],
                                           dwe[:, i, hh * 512:(hh + 1) * 512],
                                           start=(i == 0),
                                           stop=(i == NIC - 1)), tMM)
                    wait("vector", tMM, tok)
                    if ei == 0:
                        last_r2 = inc("vector", lambda e, gtt=gtt, hh=hh, b=b:
                                      e.tensor_tensor(
                                          R2[:, gtt, hh * 512:(hh + 1) * 512],
                                          R2[:, gtt, hh * 512:(hh + 1) * 512],
                                          b[:, :], op=ALU.add), vA)
                    else:
                        last_r2 = inc("vector", lambda e, gtt=gtt, hh=hh, b=b,
                                      ei=ei: e.scalar_tensor_tensor(
                                          R2[:, gtt, hh * 512:(hh + 1) * 512],
                                          b[:, :], wrt[:, gtt, ei - 1:ei],
                                          R2[:, gtt, hh * 512:(hh + 1) * 512],
                                          op0=ALU.mult, op1=ALU.add), vA)
                    bset(b, vA, last_r2)
                    if ei == NE and hh == 1:
                        wait("sync", vA, last_r2)
                        inc("sync", lambda e, gtt=gtt: e.dma_start(
                            out=out_d[gtt * 128:(gtt + 1) * 128, :],
                            in_=R2[:, gtt, :]), dO, 16)
            act_free_mm = tMM.n
        dwe_rel = tMM.n
        if upto == ("ffn%d" % ei):
            dump_and_stop()

    if not dbg["stopped"]:
        wait("sync", dO, 16 * NTT)

    with nc.Block() as block:
        @block.sync
        def _(eng):
            for f in prog["sync"]:
                f(eng)

        @block.gpsimd
        def _(eng):
            for f in prog["gpsimd"]:
                f(eng)

        @block.tensor
        def _(eng):
            for f in prog["tensor"]:
                f(eng)

        @block.vector
        def _(eng):
            for f in prog["vector"]:
                f(eng)

        @block.scalar
        def _(eng):
            for f in prog["scalar"]:
                f(eng)

    return nc


def kernel(L0_attn_x, L0_attn_Wq, L0_attn_Wk, L0_attn_Wv, L0_ffn_shared_gate,
           L0_ffn_shared_up, L0_ffn_shared_down, L0_ffn_gate_w, L0_ffn_up_w,
           L0_ffn_down_w, L0_ffn_router_w, L0_ffn_routing_bias):
    x = np.ascontiguousarray(np.asarray(L0_attn_x, dtype=np.float32))
    wq = np.ascontiguousarray(np.asarray(L0_attn_Wq, dtype=np.float32))
    wk = np.ascontiguousarray(np.asarray(L0_attn_Wk, dtype=np.float32))
    wv = np.ascontiguousarray(np.asarray(L0_attn_Wv, dtype=np.float32))
    sgwT = np.ascontiguousarray(np.asarray(L0_ffn_shared_gate, dtype=np.float32).T)
    suwT = np.ascontiguousarray(np.asarray(L0_ffn_shared_up, dtype=np.float32).T)
    sdwT = np.ascontiguousarray(np.asarray(L0_ffn_shared_down, dtype=np.float32).T)
    gw = np.ascontiguousarray(np.asarray(L0_ffn_gate_w, dtype=np.float32))
    uw = np.ascontiguousarray(np.asarray(L0_ffn_up_w, dtype=np.float32))
    dw = np.ascontiguousarray(np.asarray(L0_ffn_down_w, dtype=np.float32))
    rwT = np.ascontiguousarray(np.asarray(L0_ffn_router_w, dtype=np.float32).T)
    rbias = np.ascontiguousarray(np.broadcast_to(
        np.asarray(L0_ffn_routing_bias, dtype=np.float32), (128, NE)))
    ident = np.eye(128, dtype=np.float32)
    pp = np.arange(128)
    dmask = np.where(pp[None, :] <= pp[:, None], 0.0, NEG_BIG).astype(np.float32)

    nc = build_program()
    in_maps = []
    for c in range(8):
        in_maps.append({
            "x": np.ascontiguousarray(x[c]), "wq": wq, "wk": wk, "wv": wv,
            "sgwT": sgwT, "suwT": suwT, "sdwT": sdwT, "gw": gw, "uw": uw,
            "dw": dw, "rwT": rwT, "rbias": rbias, "ident": ident,
            "dmask": dmask,
        })
    res = run_bass_kernel_spmd(nc, in_maps, core_ids=list(range(8)))
    out = np.stack([res.results[c]["out"] for c in range(8)], axis=0)
    return out.astype(np.float32)


# version 28
# speedup vs baseline: 1.1100x; 1.0156x over previous
import numpy as np
import concourse.bass as bass
import concourse.mybir as mybir
from concourse.bass_utils import run_bass_kernel_spmd
from concourse.hw_specs import get_activation_tables

F32 = mybir.dt.float32
BF16 = mybir.dt.bfloat16
AF = mybir.ActivationFunctionType
ALU = mybir.AluOpType
AX = mybir.AxisListType

S = 2048
D = 1024
I = 2048
NE = 8
NTT = S // 128
NHC = D // 128
NIC = I // 128
NEG_BIG = -30000.0


class Ctr:
    def __init__(self, nc, name):
        self.h = nc.alloc_semaphore(name)
        self.n = 0


def build_program(upto=None):
    nc = bass.Bass("TRN2", target_bir_lowering=False)

    x_d = nc.dram_tensor("x", [S, D], F32, kind="ExternalInput").ap()
    wq_d = nc.dram_tensor("wq", [D, D], F32, kind="ExternalInput").ap()
    wk_d = nc.dram_tensor("wk", [D, D], F32, kind="ExternalInput").ap()
    wv_d = nc.dram_tensor("wv", [D, D], F32, kind="ExternalInput").ap()
    sgwT_d = nc.dram_tensor("sgwT", [D, I], F32, kind="ExternalInput").ap()
    suwT_d = nc.dram_tensor("suwT", [D, I], F32, kind="ExternalInput").ap()
    sdwT_d = nc.dram_tensor("sdwT", [I, D], F32, kind="ExternalInput").ap()
    gw_d = nc.dram_tensor("gw", [NE, D, I], F32, kind="ExternalInput").ap()
    uw_d = nc.dram_tensor("uw", [NE, D, I], F32, kind="ExternalInput").ap()
    dw_d = nc.dram_tensor("dw", [NE, I, D], F32, kind="ExternalInput").ap()
    rwT_d = nc.dram_tensor("rwT", [D, NE], F32, kind="ExternalInput").ap()
    rbias_d = nc.dram_tensor("rbias", [128, NE], F32, kind="ExternalInput").ap()
    ident_d = nc.dram_tensor("ident", [128, 128], F32, kind="ExternalInput").ap()
    dmask_d = nc.dram_tensor("dmask", [128, 128], F32, kind="ExternalInput").ap()
    out_d = nc.dram_tensor("out", [S, D], F32, kind="ExternalOutput").ap()

    R2 = nc.alloc_sbuf_tensor("R2", [128, NTT, D], F32)
    XQ = nc.alloc_sbuf_tensor("XQ", [128, NHC, S], BF16)
    KBUF = nc.alloc_sbuf_tensor("KBUF", [128, 16384], BF16)
    VBUF = nc.alloc_sbuf_tensor("VBUF", [128, 16384], BF16)
    WBUF = nc.alloc_sbuf_tensor("WBUF", [128, 8192], BF16)
    sS = nc.alloc_sbuf_tensor("sS", [128, 5760], BF16)
    sP = nc.alloc_sbuf_tensor("sP", [128, 2, S], BF16)
    sPT = nc.alloc_sbuf_tensor("sPT", [128, 8, 128], BF16)
    xlntm = nc.alloc_sbuf_tensor("xlntm", [128, 2, D], BF16)
    gscr = nc.alloc_sbuf_tensor("gscr", [128, 2, 512], BF16)
    ident = nc.alloc_sbuf_tensor("identsb", [128, 128], BF16)
    dmask = nc.alloc_sbuf_tensor("dmasksb", [128, 128], BF16)
    rwTs = nc.alloc_sbuf_tensor("rwTs", [128, NHC, NE], BF16)
    rbias = nc.alloc_sbuf_tensor("rbiassb", [128, NE], F32)
    lgt = nc.alloc_sbuf_tensor("lgt", [128, NTT, NE], F32)
    prb = nc.alloc_sbuf_tensor("prb", [128, NTT, NE], F32)
    eqb = nc.alloc_sbuf_tensor("eqb", [128, NTT, NE], F32)
    selb = nc.alloc_sbuf_tensor("selb", [128, NTT, NE], F32)
    rst = nc.alloc_sbuf_tensor("rst", [128, 4, NTT], F32)
    wrt = nc.alloc_sbuf_tensor("wrt", [128, NTT, NE], F32)
    qst = nc.alloc_sbuf_tensor("qst", [128, 4, NTT], F32)
    lst = nc.alloc_sbuf_tensor("lst", [128, 5, NTT], F32)
    cst = nc.alloc_sbuf_tensor("cst", [128, 4], F32)
    qcm = nc.alloc_sbuf_tensor("qcm", [128, 3, 4], F32)

    _sS_off = (0, 2048, 3840)

    def sSs(qi, a, b):
        o = _sS_off[qi % 3]
        return sS[:, o + a:o + b]

    kfm = KBUF[:, :].rearrange("p (c t) -> p c t", c=NHC)
    act = KBUF[:, :].rearrange("p (c t) -> p c t", c=NIC)
    vtm = VBUF[:, :].rearrange("p (c t) -> p c t", c=NTT)
    dwe = VBUF[:, :].rearrange("p (c t) -> p c t", c=NIC)
    wqkv = WBUF[:, :].rearrange("p (c t) -> p c t", c=NHC)
    wgrp = WBUF[:, :].rearrange("p (b g c t) -> p b g c t", b=2, g=2, c=NHC)

    psA = nc.alloc_psum_tensor("psA", [128, 512], F32)
    psB = nc.alloc_psum_tensor("psB", [128, 512], F32)
    psC = nc.alloc_psum_tensor("psC", [128, 512], F32)
    psD = nc.alloc_psum_tensor("psD", [128, 512], F32)
    psE = nc.alloc_psum_tensor("psE", [128, 512], F32)
    psF = nc.alloc_psum_tensor("psF", [128, 512], F32)
    ptA = nc.alloc_psum_tensor("ptA", [128, 256], BF16)
    ptB = nc.alloc_psum_tensor("ptB", [128, 256], BF16)

    ptb = [ptA[:, :], ptB[:, :],
           psE[:, :].bitcast(BF16)[:, 0:256], psF[:, :].bitcast(BF16)[:, 0:256]]

    dWc = Ctr(nc, "dWc")
    dWw = Ctr(nc, "dWw")
    dWp = Ctr(nc, "dWp")
    dWg = [Ctr(nc, "dWg0"), Ctr(nc, "dWg1")]
    dWu = [Ctr(nc, "dWu0"), Ctr(nc, "dWu1")]
    dWd = Ctr(nc, "dWd")
    dXc = [Ctr(nc, "dXc%d" % j) for j in range(4)]
    dO = Ctr(nc, "dO")
    tMM = Ctr(nc, "tMM")
    tTR = Ctr(nc, "tTR")
    vA = Ctr(nc, "vA")
    sA = Ctr(nc, "sA")

    prog = {k: [] for k in ("sync", "gpsimd", "tensor", "vector", "scalar")}
    dbg = {"stopped": False}

    def em(engine, fn):
        if not dbg["stopped"]:
            prog[engine].append(fn)

    def wait(engine, ctr, tok):
        if tok > 0:
            em(engine, lambda eng, c=ctr, t=tok: eng.wait_ge(c.h, t))

    def inc(engine, build, ctr, k=1):
        ctr.n += k
        tok = ctr.n
        em(engine, lambda eng, b=build, c=ctr, kk=k: b(eng).then_inc(c.h, kk))
        return tok

    rel = {}

    def bwait(engine, bank):
        r = rel.get(id(bank))
        if r is not None:
            wait(engine, r[0], r[1])

    def bset(bank, ctr, tok):
        rel[id(bank)] = (ctr, tok)

    def dump_and_stop():
        wait("sync", vA, vA.n)
        wait("sync", sA, sA.n)
        wait("sync", tMM, tMM.n)
        wait("sync", tTR, tTR.n)
        n = 0
        for name, buf, shape, dt in (
                ("d_r2", R2, [128, NTT * D], F32),
                ("d_xq", XQ, [128, NHC * S], BF16),
                ("d_k", KBUF, [128, 16384], BF16),
                ("d_v", VBUF, [128, 16384], BF16),
                ("d_qst", qst, [128, 4 * NTT], F32),
                ("d_lst", lst, [128, 5 * NTT], F32),
                ("d_wrt", wrt, [128, NTT * NE], F32),
                ("d_prb", prb, [128, NTT * NE], F32),
        ):
            dr = nc.dram_tensor(name, shape, dt, kind="ExternalOutput").ap()
            ap = buf[:, :] if len(buf.shape) == 2 else buf[:, :, :].rearrange(
                "p a b -> p (a b)")
            inc("sync", lambda e, dr=dr, ap=ap: e.dma_start(out=dr, in_=ap),
                dO, 16)
            n += 1
        wait("sync", dO, 16 * n)
        dbg["stopped"] = True

    t_xc = []
    for j in range(4):
        t_xc.append((dXc[j], inc("sync", lambda e, j=j: e.dma_start(
            out=R2[:, 4 * j:4 * (j + 1), :],
            in_=x_d[512 * j:512 * (j + 1), :]
            .rearrange("(c p) h -> p c h", p=128)), dXc[j], 16)))
    inc("gpsimd", lambda e: e.dma_start(out=dmask[:, :], in_=dmask_d), dWc, 16)
    inc("gpsimd", lambda e: e.dma_start(out=ident[:, :], in_=ident_d), dWc, 16)
    inc("gpsimd", lambda e: e.dma_start(out=rbias[:, :], in_=rbias_d), dWc, 16)
    t_rwT = inc("gpsimd", lambda e: e.dma_start(
        out=rwTs[:, :, :], in_=rwT_d.rearrange("(c p) e -> p c e", p=128)), dWc, 16)
    t_dmask = t_ident = t_rbias = t_rwT
    t_wk = inc("gpsimd", lambda e: e.dma_start(
        out=wqkv[:, :, :], in_=wk_d.rearrange("(c p) e -> p c e", p=128)), dWw, 16)
    inc("gpsimd", lambda e: e.dma_start(
        out=sS[:, 0:4096].rearrange("p (c t) -> p c t", c=4),
        in_=wv_d[0:512, :].rearrange("(c p) e -> p c e", p=128)), dWp, 16)
    t_wv_park = inc("gpsimd", lambda e: e.dma_start(
        out=sP[:, :, :].rearrange("p a b -> p (a b)").rearrange(
            "p (c t) -> p c t", c=4),
        in_=wv_d[512:1024, :].rearrange("(c p) e -> p c e", p=128)), dWp, 16)

    em("vector", lambda e: e.memset(cst[:, 0:1], 1e-5))
    em("vector", lambda e: e.memset(cst[:, 1:2], -2.0))


    lnsq = [0] * NTT
    lnsq_prev = [0]
    lntr = [0] * NTT

    def ln_stats(tt, v_pre=0, v_dx=None):
        if v_dx is not None:
            wait("vector", v_dx[0], v_dx[1])
        wait("vector", vA, v_pre)
        r_tok = inc("vector", lambda e, tt=tt: e.reduce_sum(
            lst[:, 0, tt:tt + 1], R2[:, tt, :], axis=AX.X, negate=True), vA)
        wait("vector", vA, r_tok)
        m_tok = inc("vector", lambda e, tt=tt: e.tensor_scalar(
            lst[:, 1, tt:tt + 1], lst[:, 0, tt:tt + 1], 1.0 / D, None,
            op0=ALU.mult), vA)
        wait("scalar", vA, m_tok)
        if v_dx is not None:
            wait("scalar", v_dx[0], v_dx[1])
        wait("scalar", sA, lnsq_prev[0])
        sq_tok = inc("scalar", lambda e, tt=tt: e.activation(
            gscr[:, :, :].rearrange("p a b -> p (a b)"), R2[:, tt, :],
            AF.Square, bias=lst[:, 1, tt:tt + 1],
            scale=1.0, accum_out=lst[:, 2, tt:tt + 1]), sA)
        lnsq_prev[0] = sq_tok
        lnsq[tt] = sq_tok
        return sq_tok

    lnap = [0] * NTT

    def ln_fin_a(tt):
        wait("scalar", sA, lnsq[tt])
        l1 = inc("scalar", lambda e, tt=tt: e.activation(
            lst[:, 3, tt:tt + 1], lst[:, 2, tt:tt + 1], AF.Ln,
            bias=cst[:, 0:1], scale=1.0 / D), sA)
        wait("scalar", sA, l1)
        l2 = inc("scalar", lambda e, tt=tt: e.activation(
            lst[:, 4, tt:tt + 1], lst[:, 3, tt:tt + 1], AF.Exp,
            scale=-0.5), sA)
        wait("vector", sA, l2)
        if tt >= 2:
            wait("vector", tTR, lntr[tt - 2])
        lnap[tt] = inc("vector", lambda e, tt=tt: e.tensor_scalar(
            xlntm[:, tt % 2, :], R2[:, tt, :], lst[:, 1, tt:tt + 1],
            lst[:, 4, tt:tt + 1], op0=ALU.add, op1=ALU.mult), vA)

    def ln_fin_b(tt):
        wait("tensor", vA, lnap[tt])
        wait("tensor", dWc, t_ident)
        xq_tok = 0
        for hp in range(NHC // 2):
            pt = ptb[hp % 4]
            bwait("tensor", pt)
            tr_tok = 0
            for j in range(2):
                hc = 2 * hp + j
                tr_tok = inc("tensor", lambda e, tt=tt, hc=hc, j=j, pt=pt:
                             e.transpose(
                                 pt[:, j * 128:(j + 1) * 128],
                                 xlntm[:, tt % 2, hc * 128:(hc + 1) * 128],
                                 ident[:, :]), tTR)
            lntr[tt] = tr_tok
            wait("scalar", tTR, tr_tok)
            xq_tok = inc("scalar", lambda e, tt=tt, hp=hp, pt=pt: e.copy(
                XQ[:, 2 * hp:2 * hp + 2, tt * 128:(tt + 1) * 128],
                pt[:, :].rearrange("p (a b) -> p a b", a=2)), sA)
            bset(pt, sA, xq_tok)
        return xq_tok

    def ln_finish():
        xq_tok = 0
        for tt in range(NTT):
            ln_fin_a(tt)
            xq_tok = ln_fin_b(tt)
        return xq_tok

    for tt in range(NTT):
        ln_stats(tt, v_pre=0, v_dx=t_xc[tt // 4])
    qkv_banks = (psA, psB, psC, psD)
    bi = 0
    wait("tensor", dWw, t_wk)
    xq_ready_s = 0
    for ts in range(4):
        for tt4 in range(4):
            tt = 4 * ts + tt4
            ln_fin_a(tt)
            xq_ready_s = ln_fin_b(tt)
        wait("tensor", sA, xq_ready_s)
        for ec in range(NHC):
            b = qkv_banks[bi % 4]
            bwait("tensor", b)
            tok = 0
            for d in range(NHC):
                tok = inc("tensor", lambda e, d=d, ec=ec, ts=ts, b=b: e.matmul(
                    b[:, :], wqkv[:, d, ec * 128:(ec + 1) * 128],
                    XQ[:, d, ts * 512:(ts + 1) * 512],
                    start=(d == 0), stop=(d == NHC - 1)), tMM)
            wait("scalar", tMM, tok)
            ev = inc("scalar", lambda e, ec=ec, ts=ts, b=b: e.copy(
                kfm[:, ec, ts * 512:(ts + 1) * 512], b[:, :]), sA)
            bset(b, sA, ev)
            bi += 1
    t_k_mm = tMM.n

    wait("gpsimd", tMM, t_k_mm)
    t_wq = inc("gpsimd", lambda e: e.dma_start(
        out=wqkv[:, :, :], in_=wq_d.rearrange("(c p) e -> p c e", p=128)), dWw, 16)

    wait("tensor", dWp, t_wv_park)
    wv_lo = sS[:, 0:4096].rearrange("p (c t) -> p c t", c=4)
    wv_hi = sP[:, :, :].rearrange("p a b -> p (a b)").rearrange(
        "p (c t) -> p c t", c=4)
    for tt in range(NTT):
        for eh in range(2):
            b = qkv_banks[bi % 4]
            bwait("tensor", b)
            tok = 0
            for d in range(NHC):
                wvsrc = wv_lo if d < 4 else wv_hi
                tok = inc("tensor", lambda e, d=d, tt=tt, eh=eh, b=b,
                          wvsrc=wvsrc: e.matmul(
                    b[:, :], XQ[:, d, tt * 128:(tt + 1) * 128],
                    wvsrc[:, d % 4, eh * 512:(eh + 1) * 512],
                    start=(d == 0), stop=(d == NHC - 1)), tMM)
            wait("scalar", tMM, tok)
            ev = inc("scalar", lambda e, tt=tt, eh=eh, b=b: e.copy(
                vtm[:, tt, eh * 512:(eh + 1) * 512], b[:, :]), sA)
            bset(b, sA, ev)
            bi += 1
    t_v_mm = tMM.n

    wait("tensor", dWw, t_wq)
    qsg = sS[:, 0:4096].rearrange(
        "p (s c t) -> p s c t", s=2, c=NHC)
    xqcopy_tok = [0] * 8
    ev_last = [0] * 8
    for t8 in range(8):
        if t8 >= 2:
            wait("scalar", sA, xqcopy_tok[t8 - 2])
        for b in qkv_banks:
            bwait("tensor", b)
        for bk in range(4):
            b = qkv_banks[bk]
            tok = 0
            for eci in range(2):
                ec = 2 * bk + eci
                half = eci * 256
                for d in range(NHC):
                    tok = inc("tensor", lambda e, d=d, ec=ec, t8=t8, b=b,
                              half=half: e.matmul(
                                  b[:, half:half + 256],
                                  wqkv[:, d, ec * 128:(ec + 1) * 128],
                                  XQ[:, d, t8 * 256:(t8 + 1) * 256],
                                  start=(d == 0), stop=(d == NHC - 1)), tMM)
            wait("scalar", tMM, tok)
            ev = inc("scalar", lambda e, bk=bk, t8=t8, b=b: e.copy(
                qsg[:, t8 % 2, 2 * bk:2 * bk + 2, :],
                b[:, :].rearrange("p (a b) -> p a b", a=2)), sA)
            bset(b, sA, ev)
            ev_last[t8] = ev
        if t8 > 0:
            wait("scalar", sA, ev_last[t8 - 1])
            xqcopy_tok[t8 - 1] = inc("scalar", lambda e, t8=t8: e.copy(
                XQ[:, :, (t8 - 1) * 256:t8 * 256],
                qsg[:, (t8 - 1) % 2, :, :]), sA)
    wait("scalar", sA, ev_last[7])
    t_q_xq = inc("scalar", lambda e: e.copy(
        XQ[:, :, 7 * 256:8 * 256], qsg[:, 7 % 2, :, :]), sA)
    xqcopy_tok[7] = t_q_xq
    t_q_mm = tMM.n
    if upto == "qkv":
        dump_and_stop()

    wait("tensor", dWc, t_dmask)
    wait("tensor", sA, t_q_xq)
    wait("vector", dWc, t_dmask)

    exp_tok = [0] * NTT
    trs_tok = [0] * NTT
    evs_tok = [0] * NTT
    evv_tok = [0] * NTT
    sc_bi = [0]

    def emit_scores(qi):
        nk = qi + 1
        ncol = nk * 128
        nsteps = (ncol + 511) // 512
        for ks in range(nsteps):
            w = min(512, ncol - ks * 512)
            b = (psA, psB)[sc_bi[0] % 2]
            sc_bi[0] += 1
            bwait("tensor", b)
            tok = 0
            for ec in range(NHC):
                tok = inc("tensor", lambda e, ec=ec, qi=qi, ks=ks, w=w, b=b:
                          e.matmul(b[:, 0:w],
                                   XQ[:, ec, qi * 128:(qi + 1) * 128],
                                   kfm[:, ec, ks * 512:ks * 512 + w],
                                   start=(ec == 0), stop=(ec == NHC - 1)), tMM)
            if ks < nsteps - 1:
                wait("scalar", tMM, tok)
                if qi >= 3 and ks == 0:
                    wait("scalar", sA, exp_tok[qi - 3])
                evs_tok[qi] = inc("scalar", lambda e, qi=qi, ks=ks, w=w, b=b:
                                  e.copy(sSs(qi, ks * 512, ks * 512 + w),
                                         b[:, 0:w]), sA)
                bset(b, sA, evs_tok[qi])
                wait("vector", sA, evs_tok[qi])
                inc("vector", lambda e, qi=qi, ks=ks, w=w: e.reduce_max(
                    qcm[:, qi % 3, ks:ks + 1],
                    sSs(qi, ks * 512, ks * 512 + w), axis=AX.X), vA)
            else:
                wait("vector", tMM, tok)
                if qi >= 3:
                    wait("vector", sA, exp_tok[qi - 3])
                if w > 128:
                    inc("vector", lambda e, qi=qi, ks=ks, w=w, b=b:
                        e.tensor_copy(sSs(qi, ks * 512, ks * 512 + w - 128),
                                      b[:, 0:w - 128]), vA)
                evv_tok[qi] = inc("vector", lambda e, qi=qi, ks=ks, w=w, b=b:
                                  e.tensor_tensor(
                                      sSs(qi, qi * 128, (qi + 1) * 128),
                                      b[:, w - 128:w], dmask[:, :],
                                      op=ALU.add), vA)
                bset(b, vA, evv_tok[qi])
                wait("vector", vA, evv_tok[qi])
                evv_tok[qi] = inc("vector", lambda e, qi=qi, ks=ks, w=w:
                                  e.reduce_max(
                                      qcm[:, qi % 3, ks:ks + 1],
                                      sSs(qi, ks * 512, ks * 512 + w),
                                      axis=AX.X), vA)

    def emit_softmax(qi):
        ncol = (qi + 1) * 128
        nsteps = (ncol + 511) // 512
        wait("vector", vA, evv_tok[qi])
        m_tok = inc("vector", lambda e, qi=qi, nsteps=nsteps: e.reduce_max(
            qst[:, 0, qi:qi + 1], qcm[:, qi % 3, 0:nsteps], axis=AX.X), vA)
        wait("vector", vA, m_tok)
        negb_tok = inc("vector", lambda e, qi=qi: e.tensor_scalar(
            qst[:, 1, qi:qi + 1], qst[:, 0, qi:qi + 1], -1.0 / 32.0, None,
            op0=ALU.mult), vA)
        wait("scalar", vA, negb_tok)
        if qi >= 2:
            wait("scalar", tTR, trs_tok[qi - 2])
        exp_tok[qi] = inc("scalar", lambda e, qi=qi, ncol=ncol: e.activation(
            sP[:, qi % 2, 0:ncol], sSs(qi, 0, ncol), AF.Exp,
            bias=qst[:, 1, qi:qi + 1], scale=1.0 / 32.0,
            accum_out=qst[:, 2, qi:qi + 1]), sA)

    def emit_ptpv(qi):
        nk = qi + 1
        npair = (nk + 1) // 2
        wait("tensor", sA, exp_tok[qi])
        bwait("tensor", psC)
        bwait("tensor", psD)
        pte = {}
        tr = []
        tok_pv = 0

        def do_pair(p):
            pt = ptb[p % 4]
            bwait("tensor", pt)
            cnt = 1 if (p == npair - 1 and nk % 2 == 1) else 2
            tr_tok = 0
            for j in range(cnt):
                kn = 2 * p + j
                tr_tok = inc("tensor", lambda e, qi=qi, kn=kn, j=j, pt=pt:
                             e.transpose(pt[:, j * 128:(j + 1) * 128],
                                         sP[:, qi % 2, kn * 128:(kn + 1) * 128],
                                         ident[:, :]), tTR)
            tr.append(tr_tok)
            wait("scalar", tTR, tr_tok)
            s = (2 * p) % 8
            pte[p] = inc("scalar", lambda e, p=p, pt=pt, s=s, cnt=cnt: e.copy(
                sPT[:, s:s + cnt, :].rearrange("p a b -> p (a b)"),
                pt[:, 0:cnt * 128]), sA)
            bset(pt, sA, pte[p])

        do_pair(0)
        if npair > 1:
            do_pair(1)
        for p in range(npair):
            if p + 2 < npair:
                do_pair(p + 2)
            wait("tensor", sA, pte[p])
            cnt = 1 if (p == npair - 1 and nk % 2 == 1) else 2
            for j in range(cnt):
                kc = 2 * p + j
                for eh, bb in ((0, psC), (1, psD)):
                    tok_pv = inc("tensor", lambda e, kc=kc, eh=eh, bb=bb, nk=nk:
                                 e.matmul(bb[:, :], sPT[:, kc % 8, :],
                                          vtm[:, kc, eh * 512:(eh + 1) * 512],
                                          start=(kc == 0),
                                          stop=(kc == nk - 1)), tMM)
        trs_tok[qi] = tr[-1]
        return tok_pv

    def emit_epilogue(qi, tok_pv):
        wait("vector", sA, exp_tok[qi])
        linv_tok = inc("vector", lambda e, qi=qi: e.reciprocal(
            qst[:, 3, qi:qi + 1], qst[:, 2, qi:qi + 1]), vA)
        wait("vector", vA, linv_tok)
        wait("vector", tMM, tok_pv)
        pv = 0
        for eh, bb in ((0, psC), (1, psD)):
            pv = inc("vector", lambda e, qi=qi, eh=eh, bb=bb:
                     e.scalar_tensor_tensor(
                         R2[:, qi, eh * 512:(eh + 1) * 512], bb[:, :],
                         qst[:, 3, qi:qi + 1],
                         R2[:, qi, eh * 512:(eh + 1) * 512],
                         op0=ALU.mult, op1=ALU.add), vA)
            bset(bb, vA, pv)
        return pv

    emit_scores(0)
    emit_scores(1)
    emit_softmax(0)
    for qi in range(NTT):
        if qi + 2 < NTT:
            emit_scores(qi + 2)
        tok_pv = emit_ptpv(qi)
        if qi + 1 < NTT:
            emit_softmax(qi + 1)
        pv = emit_epilogue(qi, tok_pv)
        ln_stats(qi, v_pre=pv)
        if qi >= 1:
            ln_fin_a(qi - 1)
        if qi >= 2:
            ln_fin_b(qi - 2)
    t_attn_v = vA.n
    if upto == "attn":
        dump_and_stop()

    ln_fin_a(NTT - 1)
    xq2_ready_s = ln_fin_b(NTT - 2)
    xq2_ready_s = ln_fin_b(NTT - 1)
    if upto == "ln2":
        dump_and_stop()

    wait("tensor", sA, xq2_ready_s)
    wait("tensor", dWc, t_rwT)
    bwait("tensor", psE)
    bwait("tensor", psF)
    for tt in range(NTT):
        lb = (psE, psF)[tt % 2]
        bwait("tensor", lb)
        tok = 0
        for hc in range(NHC):
            tok = inc("tensor", lambda e, tt=tt, hc=hc, lb=lb: e.matmul(
                lb[:, 0:NE], XQ[:, hc, tt * 128:(tt + 1) * 128], rwTs[:, hc, :],
                start=(hc == 0), stop=(hc == NHC - 1)), tMM)
        wait("vector", tMM, tok)
        if tt == 0:
            wait("vector", dWc, t_rbias)
        lev = inc("vector", lambda e, tt=tt, lb=lb: e.tensor_tensor(
            lgt[:, tt, :], lb[:, 0:NE], rbias[:, :], op=ALU.add), vA)
        bset(lb, vA, lev)
    wait("scalar", vA, lev)
    sg_tok = inc("scalar", lambda e: e.activation(
        prb[:, :, :].rearrange("p a b -> p (a b)"),
        lgt[:, :, :].rearrange("p a b -> p (a b)"), AF.Sigmoid), sA)
    wait("vector", sA, sg_tok)
    t1 = inc("vector", lambda e: e.reduce_max(
        rst[:, 0, :].unsqueeze(-1), prb[:, :, :], axis=AX.X), vA)
    wait("vector", vA, t1)
    last = 0
    for tt in range(NTT):
        last = inc("vector", lambda e, tt=tt: e.tensor_scalar(
            eqb[:, tt, :], prb[:, tt, :], rst[:, 0, tt:tt + 1], None,
            op0=ALU.is_ge), vA)
    wait("vector", vA, last)
    for tt in range(NTT):
        last = inc("vector", lambda e, tt=tt: e.scalar_tensor_tensor(
            eqb[:, tt, :], eqb[:, tt, :], cst[:, 1:2], prb[:, tt, :],
            op0=ALU.mult, op1=ALU.add), vA)
    wait("vector", vA, last)
    t2 = inc("vector", lambda e: e.reduce_max(
        rst[:, 1, :].unsqueeze(-1), eqb[:, :, :], axis=AX.X), vA)
    wait("vector", vA, t2)
    for tt in range(NTT):
        last = inc("vector", lambda e, tt=tt: e.tensor_scalar(
            selb[:, tt, :], prb[:, tt, :], rst[:, 1, tt:tt + 1], None,
            op0=ALU.is_ge), vA)
    wait("vector", vA, last)
    t3 = inc("vector", lambda e: e.tensor_mul(
        selb[:, :, :].rearrange("p a b -> p (a b)"),
        selb[:, :, :].rearrange("p a b -> p (a b)"),
        prb[:, :, :].rearrange("p a b -> p (a b)")), vA)
    wait("vector", vA, t3)
    t4 = inc("vector", lambda e: e.reduce_sum(
        rst[:, 2, :].unsqueeze(-1), selb[:, :, :], axis=AX.X), vA)
    wait("vector", vA, t4)
    t5 = inc("vector", lambda e: e.reciprocal(rst[:, 3, :], rst[:, 2, :]), vA)
    wait("vector", vA, t5)
    for tt in range(NTT):
        last = inc("vector", lambda e, tt=tt: e.tensor_scalar(
            wrt[:, tt, :], selb[:, tt, :], rst[:, 3, tt:tt + 1], None,
            op0=ALU.mult), vA)
    wait("vector", vA, last)
    t_rout_v = vA.n
    t_rout_mm = tMM.n
    if upto == "rout":
        dump_and_stop()

    gate_srcs = [sgwT_d] + [gw_d[e] for e in range(NE)]
    up_srcs = [suwT_d] + [uw_d[e] for e in range(NE)]
    down_srcs = [sdwT_d] + [dw_d[e] for e in range(NE)]

    grp_rel = [[t_q_mm, t_q_mm], [t_q_mm, t_q_mm]]
    dwe_rel = t_rout_mm
    act_free_mm = t_rout_mm
    gsl = [0, 0]
    last_r2 = 0

    for ei in range(NE + 1):
        t_dwe = 0
        for th in range(2):
            tbase = th * 1024
            unit = 0
            act_tok = [0] * NIC
            first_act = True
            for ig in range(8):
                pp = ig % 2
                wait("gpsimd", tMM, grp_rel[pp][0])
                tg = inc("gpsimd", lambda e, ei=ei, ig=ig, pp=pp: e.dma_start(
                    out=wgrp[:, pp, 0, :, :],
                    in_=gate_srcs[ei][:, ig * 256:(ig + 1) * 256]
                    .rearrange("(c p) i -> p c i", p=128)), dWg[pp], 16)
                wait("gpsimd", tMM, grp_rel[pp][1])
                tu = inc("gpsimd", lambda e, ei=ei, ig=ig, pp=pp: e.dma_start(
                    out=wgrp[:, pp, 1, :, :],
                    in_=up_srcs[ei][:, ig * 256:(ig + 1) * 256]
                    .rearrange("(c p) i -> p c i", p=128)), dWu[pp], 16)
                if th == 0 and 2 <= ig <= 5:
                    wait("gpsimd", tMM, dwe_rel)
                    qq = ig - 2
                    t_dwe = inc("gpsimd", lambda e, ei=ei, qq=qq: e.dma_start(
                        out=dwe[:, 4 * qq:4 * (qq + 1), :],
                        in_=down_srcs[ei][512 * qq:512 * (qq + 1), :]
                        .rearrange("(c p) h -> p c h", p=128)), dWd, 16)
                wait("tensor", dWg[pp], tg)
                wait("tensor", dWu[pp], tu)
                for ic in range(2):
                    i = ig * 2 + ic
                    for ts in range(2):
                        gb = (psA, psB)[unit % 2]
                        ub = (psC, psD)[unit % 2]
                        bwait("tensor", gb)
                        bwait("tensor", ub)
                        tokg = toku = 0
                        for hc in range(NHC):
                            tokg = inc("tensor", lambda e, pp=pp, hc=hc, ic=ic,
                                       ts=ts, gb=gb, tbase=tbase: e.matmul(
                                           gb[:, :],
                                           wgrp[:, pp, 0, hc,
                                                ic * 128:(ic + 1) * 128],
                                           XQ[:, hc, tbase + ts * 512:
                                              tbase + (ts + 1) * 512],
                                           start=(hc == 0),
                                           stop=(hc == NHC - 1)), tMM)
                        for hc in range(NHC):
                            toku = inc("tensor", lambda e, pp=pp, hc=hc, ic=ic,
                                       ts=ts, ub=ub, tbase=tbase: e.matmul(
                                           ub[:, :],
                                           wgrp[:, pp, 1, hc,
                                                ic * 128:(ic + 1) * 128],
                                           XQ[:, hc, tbase + ts * 512:
                                              tbase + (ts + 1) * 512],
                                           start=(hc == 0),
                                           stop=(hc == NHC - 1)), tMM)
                        wait("scalar", tMM, tokg)
                        wait("scalar", vA, gsl[unit % 2])
                        if ei == 0 and th == 0 and unit == 0:
                            wait("scalar", sA, lnsq_prev[0])
                        sil = inc("scalar", lambda e, unit=unit, gb=gb:
                                  e.activation(gscr[:, unit % 2, :],
                                               gb[:, :], AF.Silu), sA)
                        bset(gb, sA, sil)
                        wait("vector", tMM, toku)
                        wait("vector", sA, sil)
                        if first_act:
                            wait("vector", tMM, act_free_mm)
                            first_act = False
                        mu = inc("vector", lambda e, unit=unit, i=i,
                                 ts=ts, ub=ub: e.tensor_tensor(
                                     act[:, i, ts * 512:(ts + 1) * 512],
                                     gscr[:, unit % 2, :], ub[:, :],
                                     op=ALU.mult), vA)
                        bset(ub, vA, mu)
                        gsl[unit % 2] = mu
                        act_tok[i] = mu
                        unit += 1
                grp_rel[pp][0] = tMM.n
                grp_rel[pp][1] = tMM.n
            wait("tensor", dWd, t_dwe)
            for i in range(NIC):
                wait("tensor", vA, act_tok[i])
            for tt in range(8):
                gtt = th * 8 + tt
                for hh in range(2):
                    b = (psE, psF)[(tt * 2 + hh) % 2]
                    bwait("tensor", b)
                    tok = 0
                    for i in range(NIC):
                        tok = inc("tensor", lambda e, i=i, tt=tt, hh=hh, b=b:
                                  e.matmul(b[:, :],
                                           act[:, i, tt * 128:(tt + 1) * 128],
                                           dwe[:, i, hh * 512:(hh + 1) * 512],
                                           start=(i == 0),
                                           stop=(i == NIC - 1)), tMM)
                    wait("vector", tMM, tok)
                    if ei == 0:
                        last_r2 = inc("vector", lambda e, gtt=gtt, hh=hh, b=b:
                                      e.tensor_tensor(
                                          R2[:, gtt, hh * 512:(hh + 1) * 512],
                                          R2[:, gtt, hh * 512:(hh + 1) * 512],
                                          b[:, :], op=ALU.add), vA)
                    else:
                        last_r2 = inc("vector", lambda e, gtt=gtt, hh=hh, b=b,
                                      ei=ei: e.scalar_tensor_tensor(
                                          R2[:, gtt, hh * 512:(hh + 1) * 512],
                                          b[:, :], wrt[:, gtt, ei - 1:ei],
                                          R2[:, gtt, hh * 512:(hh + 1) * 512],
                                          op0=ALU.mult, op1=ALU.add), vA)
                    bset(b, vA, last_r2)
                    if ei == NE and hh == 1:
                        wait("sync", vA, last_r2)
                        inc("sync", lambda e, gtt=gtt: e.dma_start(
                            out=out_d[gtt * 128:(gtt + 1) * 128, :],
                            in_=R2[:, gtt, :]), dO, 16)
            act_free_mm = tMM.n
        dwe_rel = tMM.n
        if upto == ("ffn%d" % ei):
            dump_and_stop()

    if not dbg["stopped"]:
        wait("sync", dO, 16 * NTT)

    with nc.Block() as block:
        @block.sync
        def _(eng):
            for f in prog["sync"]:
                f(eng)

        @block.gpsimd
        def _(eng):
            for f in prog["gpsimd"]:
                f(eng)

        @block.tensor
        def _(eng):
            for f in prog["tensor"]:
                f(eng)

        @block.vector
        def _(eng):
            for f in prog["vector"]:
                f(eng)

        @block.scalar
        def _(eng):
            for f in prog["scalar"]:
                f(eng)

    return nc


def kernel(L0_attn_x, L0_attn_Wq, L0_attn_Wk, L0_attn_Wv, L0_ffn_shared_gate,
           L0_ffn_shared_up, L0_ffn_shared_down, L0_ffn_gate_w, L0_ffn_up_w,
           L0_ffn_down_w, L0_ffn_router_w, L0_ffn_routing_bias):
    x = np.ascontiguousarray(np.asarray(L0_attn_x, dtype=np.float32))
    wq = np.ascontiguousarray(np.asarray(L0_attn_Wq, dtype=np.float32))
    wk = np.ascontiguousarray(np.asarray(L0_attn_Wk, dtype=np.float32))
    wv = np.ascontiguousarray(np.asarray(L0_attn_Wv, dtype=np.float32))
    sgwT = np.ascontiguousarray(np.asarray(L0_ffn_shared_gate, dtype=np.float32).T)
    suwT = np.ascontiguousarray(np.asarray(L0_ffn_shared_up, dtype=np.float32).T)
    sdwT = np.ascontiguousarray(np.asarray(L0_ffn_shared_down, dtype=np.float32).T)
    gw = np.ascontiguousarray(np.asarray(L0_ffn_gate_w, dtype=np.float32))
    uw = np.ascontiguousarray(np.asarray(L0_ffn_up_w, dtype=np.float32))
    dw = np.ascontiguousarray(np.asarray(L0_ffn_down_w, dtype=np.float32))
    rwT = np.ascontiguousarray(np.asarray(L0_ffn_router_w, dtype=np.float32).T)
    rbias = np.ascontiguousarray(np.broadcast_to(
        np.asarray(L0_ffn_routing_bias, dtype=np.float32), (128, NE)))
    ident = np.eye(128, dtype=np.float32)
    pp = np.arange(128)
    dmask = np.where(pp[None, :] <= pp[:, None], 0.0, NEG_BIG).astype(np.float32)

    nc = build_program()
    in_maps = []
    for c in range(8):
        in_maps.append({
            "x": np.ascontiguousarray(x[c]), "wq": wq, "wk": wk, "wv": wv,
            "sgwT": sgwT, "suwT": suwT, "sdwT": sdwT, "gw": gw, "uw": uw,
            "dw": dw, "rwT": rwT, "rbias": rbias, "ident": ident,
            "dmask": dmask,
        })
    res = run_bass_kernel_spmd(nc, in_maps, core_ids=list(range(8)))
    out = np.stack([res.results[c]["out"] for c in range(8)], axis=0)
    return out.astype(np.float32)


# version 29 (final)
# speedup vs baseline: 1.1149x; 1.0044x over previous
import numpy as np
import concourse.bass as bass
import concourse.mybir as mybir
from concourse.bass_utils import run_bass_kernel_spmd
from concourse.hw_specs import get_activation_tables

F32 = mybir.dt.float32
BF16 = mybir.dt.bfloat16
AF = mybir.ActivationFunctionType
ALU = mybir.AluOpType
AX = mybir.AxisListType

S = 2048
D = 1024
I = 2048
NE = 8
NTT = S // 128
NHC = D // 128
NIC = I // 128
NEG_BIG = -30000.0


class Ctr:
    def __init__(self, nc, name):
        self.h = nc.alloc_semaphore(name)
        self.n = 0


def build_program(upto=None):
    nc = bass.Bass("TRN2", target_bir_lowering=False)

    x_d = nc.dram_tensor("x", [S, D], F32, kind="ExternalInput").ap()
    wq_d = nc.dram_tensor("wq", [D, D], F32, kind="ExternalInput").ap()
    wk_d = nc.dram_tensor("wk", [D, D], F32, kind="ExternalInput").ap()
    wv_d = nc.dram_tensor("wv", [D, D], F32, kind="ExternalInput").ap()
    sgwT_d = nc.dram_tensor("sgwT", [D, I], F32, kind="ExternalInput").ap()
    suwT_d = nc.dram_tensor("suwT", [D, I], F32, kind="ExternalInput").ap()
    sdwT_d = nc.dram_tensor("sdwT", [I, D], F32, kind="ExternalInput").ap()
    gw_d = nc.dram_tensor("gw", [NE, D, I], F32, kind="ExternalInput").ap()
    uw_d = nc.dram_tensor("uw", [NE, D, I], F32, kind="ExternalInput").ap()
    dw_d = nc.dram_tensor("dw", [NE, I, D], F32, kind="ExternalInput").ap()
    rwT_d = nc.dram_tensor("rwT", [D, NE], F32, kind="ExternalInput").ap()
    rbias_d = nc.dram_tensor("rbias", [128, NE], F32, kind="ExternalInput").ap()
    ident_d = nc.dram_tensor("ident", [128, 128], F32, kind="ExternalInput").ap()
    dmask_d = nc.dram_tensor("dmask", [128, 128], F32, kind="ExternalInput").ap()
    out_d = nc.dram_tensor("out", [S, D], F32, kind="ExternalOutput").ap()

    R2 = nc.alloc_sbuf_tensor("R2", [128, NTT, D], F32)
    XQ = nc.alloc_sbuf_tensor("XQ", [128, NHC, S], BF16)
    KBUF = nc.alloc_sbuf_tensor("KBUF", [128, 16384], BF16)
    VBUF = nc.alloc_sbuf_tensor("VBUF", [128, 16384], BF16)
    WBUF = nc.alloc_sbuf_tensor("WBUF", [128, 8192], BF16)
    sS = nc.alloc_sbuf_tensor("sS", [128, 5760], BF16)
    sP = nc.alloc_sbuf_tensor("sP", [128, 2, S], BF16)
    sPT = nc.alloc_sbuf_tensor("sPT", [128, 8, 128], BF16)
    xlntm = nc.alloc_sbuf_tensor("xlntm", [128, 2, D], BF16)
    gscr = nc.alloc_sbuf_tensor("gscr", [128, 2, 512], BF16)
    ident = nc.alloc_sbuf_tensor("identsb", [128, 128], BF16)
    dmask = nc.alloc_sbuf_tensor("dmasksb", [128, 128], BF16)
    rwTs = nc.alloc_sbuf_tensor("rwTs", [128, NHC, NE], BF16)
    rbias = nc.alloc_sbuf_tensor("rbiassb", [128, NE], F32)
    lgt = nc.alloc_sbuf_tensor("lgt", [128, NTT, NE], F32)
    prb = nc.alloc_sbuf_tensor("prb", [128, NTT, NE], F32)
    eqb = nc.alloc_sbuf_tensor("eqb", [128, NTT, NE], F32)
    selb = nc.alloc_sbuf_tensor("selb", [128, NTT, NE], F32)
    rst = nc.alloc_sbuf_tensor("rst", [128, 4, NTT], F32)
    wrt = nc.alloc_sbuf_tensor("wrt", [128, NTT, NE], F32)
    qst = nc.alloc_sbuf_tensor("qst", [128, 4, NTT], F32)
    lst = nc.alloc_sbuf_tensor("lst", [128, 5, NTT], F32)
    cst = nc.alloc_sbuf_tensor("cst", [128, 4], F32)
    qcm = nc.alloc_sbuf_tensor("qcm", [128, 3, 4], F32)

    _sS_off = (0, 2048, 3840)

    def sSs(qi, a, b):
        o = _sS_off[qi % 3]
        return sS[:, o + a:o + b]

    kfm = KBUF[:, :].rearrange("p (c t) -> p c t", c=NHC)
    act = KBUF[:, :].rearrange("p (c t) -> p c t", c=NIC)
    vtm = VBUF[:, :].rearrange("p (c t) -> p c t", c=NTT)
    dwe = VBUF[:, :].rearrange("p (c t) -> p c t", c=NIC)
    wqkv = WBUF[:, :].rearrange("p (c t) -> p c t", c=NHC)
    wgrp = WBUF[:, :].rearrange("p (b g c t) -> p b g c t", b=2, g=2, c=NHC)

    psA = nc.alloc_psum_tensor("psA", [128, 512], F32)
    psB = nc.alloc_psum_tensor("psB", [128, 512], F32)
    psC = nc.alloc_psum_tensor("psC", [128, 512], F32)
    psD = nc.alloc_psum_tensor("psD", [128, 512], F32)
    psE = nc.alloc_psum_tensor("psE", [128, 512], F32)
    psF = nc.alloc_psum_tensor("psF", [128, 512], F32)
    ptA = nc.alloc_psum_tensor("ptA", [128, 256], BF16)
    ptB = nc.alloc_psum_tensor("ptB", [128, 256], BF16)

    ptb = [ptA[:, :], ptB[:, :],
           psE[:, :].bitcast(BF16)[:, 0:256], psF[:, :].bitcast(BF16)[:, 0:256]]

    dWc = Ctr(nc, "dWc")
    dWw = Ctr(nc, "dWw")
    dWp = Ctr(nc, "dWp")
    dWg = [Ctr(nc, "dWg0"), Ctr(nc, "dWg1")]
    dWu = [Ctr(nc, "dWu0"), Ctr(nc, "dWu1")]
    dWd = Ctr(nc, "dWd")
    dXc = [Ctr(nc, "dXc%d" % j) for j in range(4)]
    dO = Ctr(nc, "dO")
    tMM = Ctr(nc, "tMM")
    tTR = Ctr(nc, "tTR")
    vA = Ctr(nc, "vA")
    sA = Ctr(nc, "sA")

    prog = {k: [] for k in ("sync", "gpsimd", "tensor", "vector", "scalar")}
    dbg = {"stopped": False}

    def em(engine, fn):
        if not dbg["stopped"]:
            prog[engine].append(fn)

    def wait(engine, ctr, tok):
        if tok > 0:
            em(engine, lambda eng, c=ctr, t=tok: eng.wait_ge(c.h, t))

    def inc(engine, build, ctr, k=1):
        ctr.n += k
        tok = ctr.n
        em(engine, lambda eng, b=build, c=ctr, kk=k: b(eng).then_inc(c.h, kk))
        return tok

    rel = {}

    def bwait(engine, bank):
        r = rel.get(id(bank))
        if r is not None:
            wait(engine, r[0], r[1])

    def bset(bank, ctr, tok):
        rel[id(bank)] = (ctr, tok)

    def dump_and_stop():
        wait("sync", vA, vA.n)
        wait("sync", sA, sA.n)
        wait("sync", tMM, tMM.n)
        wait("sync", tTR, tTR.n)
        n = 0
        for name, buf, shape, dt in (
                ("d_r2", R2, [128, NTT * D], F32),
                ("d_xq", XQ, [128, NHC * S], BF16),
                ("d_k", KBUF, [128, 16384], BF16),
                ("d_v", VBUF, [128, 16384], BF16),
                ("d_qst", qst, [128, 4 * NTT], F32),
                ("d_lst", lst, [128, 5 * NTT], F32),
                ("d_wrt", wrt, [128, NTT * NE], F32),
                ("d_prb", prb, [128, NTT * NE], F32),
        ):
            dr = nc.dram_tensor(name, shape, dt, kind="ExternalOutput").ap()
            ap = buf[:, :] if len(buf.shape) == 2 else buf[:, :, :].rearrange(
                "p a b -> p (a b)")
            inc("sync", lambda e, dr=dr, ap=ap: e.dma_start(out=dr, in_=ap),
                dO, 16)
            n += 1
        wait("sync", dO, 16 * n)
        dbg["stopped"] = True

    t_xc = []
    for j in range(4):
        t_xc.append((dXc[j], inc("sync", lambda e, j=j: e.dma_start(
            out=R2[:, 4 * j:4 * (j + 1), :],
            in_=x_d[512 * j:512 * (j + 1), :]
            .rearrange("(c p) h -> p c h", p=128)), dXc[j], 16)))
    inc("gpsimd", lambda e: e.dma_start(out=dmask[:, :], in_=dmask_d), dWc, 16)
    inc("gpsimd", lambda e: e.dma_start(out=ident[:, :], in_=ident_d), dWc, 16)
    inc("gpsimd", lambda e: e.dma_start(out=rbias[:, :], in_=rbias_d), dWc, 16)
    t_rwT = inc("gpsimd", lambda e: e.dma_start(
        out=rwTs[:, :, :], in_=rwT_d.rearrange("(c p) e -> p c e", p=128)), dWc, 16)
    t_dmask = t_ident = t_rbias = t_rwT
    t_wk = inc("gpsimd", lambda e: e.dma_start(
        out=wqkv[:, :, :], in_=wk_d.rearrange("(c p) e -> p c e", p=128)), dWw, 16)
    inc("gpsimd", lambda e: e.dma_start(
        out=sS[:, 0:4096].rearrange("p (c t) -> p c t", c=4),
        in_=wv_d[0:512, :].rearrange("(c p) e -> p c e", p=128)), dWp, 16)
    t_wv_park = inc("gpsimd", lambda e: e.dma_start(
        out=sP[:, :, :].rearrange("p a b -> p (a b)").rearrange(
            "p (c t) -> p c t", c=4),
        in_=wv_d[512:1024, :].rearrange("(c p) e -> p c e", p=128)), dWp, 16)

    em("vector", lambda e: e.memset(cst[:, 0:1], 1e-5))
    em("vector", lambda e: e.memset(cst[:, 1:2], -2.0))


    lnsq = [0] * NTT
    lnsq_prev = [0]
    lntr = [0] * NTT

    def ln_stats(tt, v_pre=0, v_dx=None):
        if v_dx is not None:
            wait("vector", v_dx[0], v_dx[1])
        wait("vector", vA, v_pre)
        r_tok = inc("vector", lambda e, tt=tt: e.reduce_sum(
            lst[:, 0, tt:tt + 1], R2[:, tt, :], axis=AX.X, negate=True), vA)
        wait("vector", vA, r_tok)
        m_tok = inc("vector", lambda e, tt=tt: e.tensor_scalar(
            lst[:, 1, tt:tt + 1], lst[:, 0, tt:tt + 1], 1.0 / D, None,
            op0=ALU.mult), vA)
        wait("scalar", vA, m_tok)
        if v_dx is not None:
            wait("scalar", v_dx[0], v_dx[1])
        wait("scalar", sA, lnsq_prev[0])
        sq_tok = inc("scalar", lambda e, tt=tt: e.activation(
            gscr[:, :, :].rearrange("p a b -> p (a b)"), R2[:, tt, :],
            AF.Square, bias=lst[:, 1, tt:tt + 1],
            scale=1.0, accum_out=lst[:, 2, tt:tt + 1]), sA)
        lnsq_prev[0] = sq_tok
        lnsq[tt] = sq_tok
        return sq_tok

    lnap = [0] * NTT

    def ln_fin_a(tt):
        wait("scalar", sA, lnsq[tt])
        l1 = inc("scalar", lambda e, tt=tt: e.activation(
            lst[:, 3, tt:tt + 1], lst[:, 2, tt:tt + 1], AF.Ln,
            bias=cst[:, 0:1], scale=1.0 / D), sA)
        wait("scalar", sA, l1)
        l2 = inc("scalar", lambda e, tt=tt: e.activation(
            lst[:, 4, tt:tt + 1], lst[:, 3, tt:tt + 1], AF.Exp,
            scale=-0.5), sA)
        wait("vector", sA, l2)
        if tt >= 2:
            wait("vector", tTR, lntr[tt - 2])
        lnap[tt] = inc("vector", lambda e, tt=tt: e.tensor_scalar(
            xlntm[:, tt % 2, :], R2[:, tt, :], lst[:, 1, tt:tt + 1],
            lst[:, 4, tt:tt + 1], op0=ALU.add, op1=ALU.mult), vA)

    def ln_fin_b(tt):
        wait("tensor", vA, lnap[tt])
        wait("tensor", dWc, t_ident)
        xq_tok = 0
        for hp in range(NHC // 2):
            pt = ptb[hp % 4]
            bwait("tensor", pt)
            tr_tok = 0
            for j in range(2):
                hc = 2 * hp + j
                tr_tok = inc("tensor", lambda e, tt=tt, hc=hc, j=j, pt=pt:
                             e.transpose(
                                 pt[:, j * 128:(j + 1) * 128],
                                 xlntm[:, tt % 2, hc * 128:(hc + 1) * 128],
                                 ident[:, :]), tTR)
            lntr[tt] = tr_tok
            wait("scalar", tTR, tr_tok)
            xq_tok = inc("scalar", lambda e, tt=tt, hp=hp, pt=pt: e.copy(
                XQ[:, 2 * hp:2 * hp + 2, tt * 128:(tt + 1) * 128],
                pt[:, :].rearrange("p (a b) -> p a b", a=2)), sA)
            bset(pt, sA, xq_tok)
        return xq_tok

    def ln_finish():
        xq_tok = 0
        for tt in range(NTT):
            ln_fin_a(tt)
            xq_tok = ln_fin_b(tt)
        return xq_tok

    lnxq = [0] * NTT
    for tt in range(NTT):
        ln_stats(tt, v_pre=0, v_dx=t_xc[tt // 4])
        if tt >= 1:
            ln_fin_a(tt - 1)
        if tt >= 2:
            lnxq[tt - 2] = ln_fin_b(tt - 2)
    ln_fin_a(NTT - 1)
    lnxq[NTT - 2] = ln_fin_b(NTT - 2)
    lnxq[NTT - 1] = ln_fin_b(NTT - 1)
    xq_ready_s = lnxq[NTT - 1]
    qkv_banks = (psA, psB, psC, psD)
    bi = 0
    wait("tensor", dWw, t_wk)
    for ts in range(4):
        wait("tensor", sA, lnxq[4 * ts + 3])
        for ec in range(NHC):
            b = qkv_banks[bi % 4]
            bwait("tensor", b)
            tok = 0
            for d in range(NHC):
                tok = inc("tensor", lambda e, d=d, ec=ec, ts=ts, b=b: e.matmul(
                    b[:, :], wqkv[:, d, ec * 128:(ec + 1) * 128],
                    XQ[:, d, ts * 512:(ts + 1) * 512],
                    start=(d == 0), stop=(d == NHC - 1)), tMM)
            wait("scalar", tMM, tok)
            ev = inc("scalar", lambda e, ec=ec, ts=ts, b=b: e.copy(
                kfm[:, ec, ts * 512:(ts + 1) * 512], b[:, :]), sA)
            bset(b, sA, ev)
            bi += 1
    t_k_mm = tMM.n

    wait("gpsimd", tMM, t_k_mm)
    t_wq = inc("gpsimd", lambda e: e.dma_start(
        out=wqkv[:, :, :], in_=wq_d.rearrange("(c p) e -> p c e", p=128)), dWw, 16)

    wait("tensor", dWp, t_wv_park)
    wv_lo = sS[:, 0:4096].rearrange("p (c t) -> p c t", c=4)
    wv_hi = sP[:, :, :].rearrange("p a b -> p (a b)").rearrange(
        "p (c t) -> p c t", c=4)
    for tt in range(NTT):
        for eh in range(2):
            b = qkv_banks[bi % 4]
            bwait("tensor", b)
            tok = 0
            for d in range(NHC):
                wvsrc = wv_lo if d < 4 else wv_hi
                tok = inc("tensor", lambda e, d=d, tt=tt, eh=eh, b=b,
                          wvsrc=wvsrc: e.matmul(
                    b[:, :], XQ[:, d, tt * 128:(tt + 1) * 128],
                    wvsrc[:, d % 4, eh * 512:(eh + 1) * 512],
                    start=(d == 0), stop=(d == NHC - 1)), tMM)
            wait("scalar", tMM, tok)
            ev = inc("scalar", lambda e, tt=tt, eh=eh, b=b: e.copy(
                vtm[:, tt, eh * 512:(eh + 1) * 512], b[:, :]), sA)
            bset(b, sA, ev)
            bi += 1
    t_v_mm = tMM.n

    wait("tensor", dWw, t_wq)
    qsg = sS[:, 0:4096].rearrange(
        "p (s c t) -> p s c t", s=2, c=NHC)
    xqcopy_tok = [0] * 8
    ev_last = [0] * 8
    for t8 in range(8):
        if t8 >= 2:
            wait("scalar", sA, xqcopy_tok[t8 - 2])
        for b in qkv_banks:
            bwait("tensor", b)
        for bk in range(4):
            b = qkv_banks[bk]
            tok = 0
            for eci in range(2):
                ec = 2 * bk + eci
                half = eci * 256
                for d in range(NHC):
                    tok = inc("tensor", lambda e, d=d, ec=ec, t8=t8, b=b,
                              half=half: e.matmul(
                                  b[:, half:half + 256],
                                  wqkv[:, d, ec * 128:(ec + 1) * 128],
                                  XQ[:, d, t8 * 256:(t8 + 1) * 256],
                                  start=(d == 0), stop=(d == NHC - 1)), tMM)
            wait("scalar", tMM, tok)
            ev = inc("scalar", lambda e, bk=bk, t8=t8, b=b: e.copy(
                qsg[:, t8 % 2, 2 * bk:2 * bk + 2, :],
                b[:, :].rearrange("p (a b) -> p a b", a=2)), sA)
            bset(b, sA, ev)
            ev_last[t8] = ev
        if t8 > 0:
            wait("scalar", sA, ev_last[t8 - 1])
            xqcopy_tok[t8 - 1] = inc("scalar", lambda e, t8=t8: e.copy(
                XQ[:, :, (t8 - 1) * 256:t8 * 256],
                qsg[:, (t8 - 1) % 2, :, :]), sA)
    wait("scalar", sA, ev_last[7])
    t_q_xq = inc("scalar", lambda e: e.copy(
        XQ[:, :, 7 * 256:8 * 256], qsg[:, 7 % 2, :, :]), sA)
    xqcopy_tok[7] = t_q_xq
    t_q_mm = tMM.n
    if upto == "qkv":
        dump_and_stop()

    wait("tensor", dWc, t_dmask)
    wait("tensor", sA, t_q_xq)
    wait("vector", dWc, t_dmask)

    exp_tok = [0] * NTT
    trs_tok = [0] * NTT
    evs_tok = [0] * NTT
    evv_tok = [0] * NTT
    sc_bi = [0]

    def emit_scores(qi):
        nk = qi + 1
        ncol = nk * 128
        nsteps = (ncol + 511) // 512
        for ks in range(nsteps):
            w = min(512, ncol - ks * 512)
            b = (psA, psB)[sc_bi[0] % 2]
            sc_bi[0] += 1
            bwait("tensor", b)
            tok = 0
            for ec in range(NHC):
                tok = inc("tensor", lambda e, ec=ec, qi=qi, ks=ks, w=w, b=b:
                          e.matmul(b[:, 0:w],
                                   XQ[:, ec, qi * 128:(qi + 1) * 128],
                                   kfm[:, ec, ks * 512:ks * 512 + w],
                                   start=(ec == 0), stop=(ec == NHC - 1)), tMM)
            if ks < nsteps - 1:
                wait("scalar", tMM, tok)
                if qi >= 3 and ks == 0:
                    wait("scalar", sA, exp_tok[qi - 3])
                evs_tok[qi] = inc("scalar", lambda e, qi=qi, ks=ks, w=w, b=b:
                                  e.copy(sSs(qi, ks * 512, ks * 512 + w),
                                         b[:, 0:w]), sA)
                bset(b, sA, evs_tok[qi])
                wait("vector", sA, evs_tok[qi])
                inc("vector", lambda e, qi=qi, ks=ks, w=w: e.reduce_max(
                    qcm[:, qi % 3, ks:ks + 1],
                    sSs(qi, ks * 512, ks * 512 + w), axis=AX.X), vA)
            else:
                wait("vector", tMM, tok)
                if qi >= 3:
                    wait("vector", sA, exp_tok[qi - 3])
                if w > 128:
                    inc("vector", lambda e, qi=qi, ks=ks, w=w, b=b:
                        e.tensor_copy(sSs(qi, ks * 512, ks * 512 + w - 128),
                                      b[:, 0:w - 128]), vA)
                evv_tok[qi] = inc("vector", lambda e, qi=qi, ks=ks, w=w, b=b:
                                  e.tensor_tensor(
                                      sSs(qi, qi * 128, (qi + 1) * 128),
                                      b[:, w - 128:w], dmask[:, :],
                                      op=ALU.add), vA)
                bset(b, vA, evv_tok[qi])
                wait("vector", vA, evv_tok[qi])
                evv_tok[qi] = inc("vector", lambda e, qi=qi, ks=ks, w=w:
                                  e.reduce_max(
                                      qcm[:, qi % 3, ks:ks + 1],
                                      sSs(qi, ks * 512, ks * 512 + w),
                                      axis=AX.X), vA)

    def emit_softmax(qi):
        ncol = (qi + 1) * 128
        nsteps = (ncol + 511) // 512
        wait("vector", vA, evv_tok[qi])
        m_tok = inc("vector", lambda e, qi=qi, nsteps=nsteps: e.reduce_max(
            qst[:, 0, qi:qi + 1], qcm[:, qi % 3, 0:nsteps], axis=AX.X), vA)
        wait("vector", vA, m_tok)
        negb_tok = inc("vector", lambda e, qi=qi: e.tensor_scalar(
            qst[:, 1, qi:qi + 1], qst[:, 0, qi:qi + 1], -1.0 / 32.0, None,
            op0=ALU.mult), vA)
        wait("scalar", vA, negb_tok)
        if qi >= 2:
            wait("scalar", tTR, trs_tok[qi - 2])
        exp_tok[qi] = inc("scalar", lambda e, qi=qi, ncol=ncol: e.activation(
            sP[:, qi % 2, 0:ncol], sSs(qi, 0, ncol), AF.Exp,
            bias=qst[:, 1, qi:qi + 1], scale=1.0 / 32.0,
            accum_out=qst[:, 2, qi:qi + 1]), sA)

    def emit_ptpv(qi):
        nk = qi + 1
        npair = (nk + 1) // 2
        wait("tensor", sA, exp_tok[qi])
        bwait("tensor", psC)
        bwait("tensor", psD)
        pte = {}
        tr = []
        tok_pv = 0

        def do_pair(p):
            pt = ptb[p % 4]
            bwait("tensor", pt)
            cnt = 1 if (p == npair - 1 and nk % 2 == 1) else 2
            tr_tok = 0
            for j in range(cnt):
                kn = 2 * p + j
                tr_tok = inc("tensor", lambda e, qi=qi, kn=kn, j=j, pt=pt:
                             e.transpose(pt[:, j * 128:(j + 1) * 128],
                                         sP[:, qi % 2, kn * 128:(kn + 1) * 128],
                                         ident[:, :]), tTR)
            tr.append(tr_tok)
            wait("scalar", tTR, tr_tok)
            s = (2 * p) % 8
            pte[p] = inc("scalar", lambda e, p=p, pt=pt, s=s, cnt=cnt: e.copy(
                sPT[:, s:s + cnt, :].rearrange("p a b -> p (a b)"),
                pt[:, 0:cnt * 128]), sA)
            bset(pt, sA, pte[p])

        do_pair(0)
        if npair > 1:
            do_pair(1)
        for p in range(npair):
            if p + 2 < npair:
                do_pair(p + 2)
            wait("tensor", sA, pte[p])
            cnt = 1 if (p == npair - 1 and nk % 2 == 1) else 2
            for j in range(cnt):
                kc = 2 * p + j
                for eh, bb in ((0, psC), (1, psD)):
                    tok_pv = inc("tensor", lambda e, kc=kc, eh=eh, bb=bb, nk=nk:
                                 e.matmul(bb[:, :], sPT[:, kc % 8, :],
                                          vtm[:, kc, eh * 512:(eh + 1) * 512],
                                          start=(kc == 0),
                                          stop=(kc == nk - 1)), tMM)
        trs_tok[qi] = tr[-1]
        return tok_pv

    def emit_epilogue(qi, tok_pv):
        wait("vector", sA, exp_tok[qi])
        linv_tok = inc("vector", lambda e, qi=qi: e.reciprocal(
            qst[:, 3, qi:qi + 1], qst[:, 2, qi:qi + 1]), vA)
        wait("vector", vA, linv_tok)
        wait("vector", tMM, tok_pv)
        pv = 0
        for eh, bb in ((0, psC), (1, psD)):
            pv = inc("vector", lambda e, qi=qi, eh=eh, bb=bb:
                     e.scalar_tensor_tensor(
                         R2[:, qi, eh * 512:(eh + 1) * 512], bb[:, :],
                         qst[:, 3, qi:qi + 1],
                         R2[:, qi, eh * 512:(eh + 1) * 512],
                         op0=ALU.mult, op1=ALU.add), vA)
            bset(bb, vA, pv)
        return pv

    emit_scores(0)
    emit_scores(1)
    emit_softmax(0)
    for qi in range(NTT):
        if qi + 2 < NTT:
            emit_scores(qi + 2)
        tok_pv = emit_ptpv(qi)
        if qi + 1 < NTT:
            emit_softmax(qi + 1)
        pv = emit_epilogue(qi, tok_pv)
        ln_stats(qi, v_pre=pv)
        if qi >= 1:
            ln_fin_a(qi - 1)
        if qi >= 2:
            ln_fin_b(qi - 2)
    t_attn_v = vA.n
    if upto == "attn":
        dump_and_stop()

    ln_fin_a(NTT - 1)
    xq2_ready_s = ln_fin_b(NTT - 2)
    xq2_ready_s = ln_fin_b(NTT - 1)
    if upto == "ln2":
        dump_and_stop()

    wait("tensor", sA, xq2_ready_s)
    wait("tensor", dWc, t_rwT)
    bwait("tensor", psE)
    logit_prev = 0
    for tt in range(NTT):
        if logit_prev:
            wait("tensor", vA, logit_prev)
        tok = 0
        for hc in range(NHC):
            tok = inc("tensor", lambda e, tt=tt, hc=hc: e.matmul(
                psE[:, 0:NE], XQ[:, hc, tt * 128:(tt + 1) * 128], rwTs[:, hc, :],
                start=(hc == 0), stop=(hc == NHC - 1)), tMM)
        wait("vector", tMM, tok)
        if tt == 0:
            wait("vector", dWc, t_rbias)
        logit_prev = inc("vector", lambda e, tt=tt: e.tensor_tensor(
            lgt[:, tt, :], psE[:, 0:NE], rbias[:, :], op=ALU.add), vA)
    bset(psE, vA, logit_prev)
    wait("scalar", vA, logit_prev)
    sg_tok = inc("scalar", lambda e: e.activation(
        prb[:, :, :].rearrange("p a b -> p (a b)"),
        lgt[:, :, :].rearrange("p a b -> p (a b)"), AF.Sigmoid), sA)
    wait("vector", sA, sg_tok)
    t1 = inc("vector", lambda e: e.reduce_max(
        rst[:, 0, :].unsqueeze(-1), prb[:, :, :], axis=AX.X), vA)
    wait("vector", vA, t1)
    last = 0
    for tt in range(NTT):
        last = inc("vector", lambda e, tt=tt: e.tensor_scalar(
            eqb[:, tt, :], prb[:, tt, :], rst[:, 0, tt:tt + 1], None,
            op0=ALU.is_ge), vA)
    wait("vector", vA, last)
    for tt in range(NTT):
        last = inc("vector", lambda e, tt=tt: e.scalar_tensor_tensor(
            eqb[:, tt, :], eqb[:, tt, :], cst[:, 1:2], prb[:, tt, :],
            op0=ALU.mult, op1=ALU.add), vA)
    wait("vector", vA, last)
    t2 = inc("vector", lambda e: e.reduce_max(
        rst[:, 1, :].unsqueeze(-1), eqb[:, :, :], axis=AX.X), vA)
    wait("vector", vA, t2)
    for tt in range(NTT):
        last = inc("vector", lambda e, tt=tt: e.tensor_scalar(
            selb[:, tt, :], prb[:, tt, :], rst[:, 1, tt:tt + 1], None,
            op0=ALU.is_ge), vA)
    wait("vector", vA, last)
    t3 = inc("vector", lambda e: e.tensor_mul(
        selb[:, :, :].rearrange("p a b -> p (a b)"),
        selb[:, :, :].rearrange("p a b -> p (a b)"),
        prb[:, :, :].rearrange("p a b -> p (a b)")), vA)
    wait("vector", vA, t3)
    t4 = inc("vector", lambda e: e.reduce_sum(
        rst[:, 2, :].unsqueeze(-1), selb[:, :, :], axis=AX.X), vA)
    wait("vector", vA, t4)
    t5 = inc("vector", lambda e: e.reciprocal(rst[:, 3, :], rst[:, 2, :]), vA)
    wait("vector", vA, t5)
    for tt in range(NTT):
        last = inc("vector", lambda e, tt=tt: e.tensor_scalar(
            wrt[:, tt, :], selb[:, tt, :], rst[:, 3, tt:tt + 1], None,
            op0=ALU.mult), vA)
    wait("vector", vA, last)
    t_rout_v = vA.n
    t_rout_mm = tMM.n
    if upto == "rout":
        dump_and_stop()

    gate_srcs = [sgwT_d] + [gw_d[e] for e in range(NE)]
    up_srcs = [suwT_d] + [uw_d[e] for e in range(NE)]
    down_srcs = [sdwT_d] + [dw_d[e] for e in range(NE)]

    grp_rel = [[t_q_mm, t_q_mm], [t_q_mm, t_q_mm]]
    dwe_rel = t_rout_mm
    act_free_mm = t_rout_mm
    gsl = [0, 0]
    last_r2 = 0

    for ei in range(NE + 1):
        t_dwe = 0
        for th in range(2):
            tbase = th * 1024
            unit = 0
            act_tok = [0] * NIC
            first_act = True
            for ig in range(8):
                pp = ig % 2
                wait("gpsimd", tMM, grp_rel[pp][0])
                tg = inc("gpsimd", lambda e, ei=ei, ig=ig, pp=pp: e.dma_start(
                    out=wgrp[:, pp, 0, :, :],
                    in_=gate_srcs[ei][:, ig * 256:(ig + 1) * 256]
                    .rearrange("(c p) i -> p c i", p=128)), dWg[pp], 16)
                wait("gpsimd", tMM, grp_rel[pp][1])
                tu = inc("gpsimd", lambda e, ei=ei, ig=ig, pp=pp: e.dma_start(
                    out=wgrp[:, pp, 1, :, :],
                    in_=up_srcs[ei][:, ig * 256:(ig + 1) * 256]
                    .rearrange("(c p) i -> p c i", p=128)), dWu[pp], 16)
                if th == 0 and 2 <= ig <= 5:
                    wait("gpsimd", tMM, dwe_rel)
                    qq = ig - 2
                    t_dwe = inc("gpsimd", lambda e, ei=ei, qq=qq: e.dma_start(
                        out=dwe[:, 4 * qq:4 * (qq + 1), :],
                        in_=down_srcs[ei][512 * qq:512 * (qq + 1), :]
                        .rearrange("(c p) h -> p c h", p=128)), dWd, 16)
                wait("tensor", dWg[pp], tg)
                wait("tensor", dWu[pp], tu)
                for ic in range(2):
                    i = ig * 2 + ic
                    for ts in range(2):
                        gb = (psA, psB)[unit % 2]
                        ub = (psC, psD)[unit % 2]
                        bwait("tensor", gb)
                        bwait("tensor", ub)
                        tokg = toku = 0
                        for hc in range(NHC):
                            tokg = inc("tensor", lambda e, pp=pp, hc=hc, ic=ic,
                                       ts=ts, gb=gb, tbase=tbase: e.matmul(
                                           gb[:, :],
                                           wgrp[:, pp, 0, hc,
                                                ic * 128:(ic + 1) * 128],
                                           XQ[:, hc, tbase + ts * 512:
                                              tbase + (ts + 1) * 512],
                                           start=(hc == 0),
                                           stop=(hc == NHC - 1)), tMM)
                        for hc in range(NHC):
                            toku = inc("tensor", lambda e, pp=pp, hc=hc, ic=ic,
                                       ts=ts, ub=ub, tbase=tbase: e.matmul(
                                           ub[:, :],
                                           wgrp[:, pp, 1, hc,
                                                ic * 128:(ic + 1) * 128],
                                           XQ[:, hc, tbase + ts * 512:
                                              tbase + (ts + 1) * 512],
                                           start=(hc == 0),
                                           stop=(hc == NHC - 1)), tMM)
                        wait("scalar", tMM, tokg)
                        wait("scalar", vA, gsl[unit % 2])
                        if ei == 0 and th == 0 and unit == 0:
                            wait("scalar", sA, lnsq_prev[0])
                        sil = inc("scalar", lambda e, unit=unit, gb=gb:
                                  e.activation(gscr[:, unit % 2, :],
                                               gb[:, :], AF.Silu), sA)
                        bset(gb, sA, sil)
                        wait("vector", tMM, toku)
                        wait("vector", sA, sil)
                        if first_act:
                            wait("vector", tMM, act_free_mm)
                            first_act = False
                        mu = inc("vector", lambda e, unit=unit, i=i,
                                 ts=ts, ub=ub: e.tensor_tensor(
                                     act[:, i, ts * 512:(ts + 1) * 512],
                                     gscr[:, unit % 2, :], ub[:, :],
                                     op=ALU.mult), vA)
                        bset(ub, vA, mu)
                        gsl[unit % 2] = mu
                        act_tok[i] = mu
                        unit += 1
                grp_rel[pp][0] = tMM.n
                grp_rel[pp][1] = tMM.n
            wait("tensor", dWd, t_dwe)
            for i in range(NIC):
                wait("tensor", vA, act_tok[i])
            for tt in range(8):
                gtt = th * 8 + tt
                for hh in range(2):
                    b = (psE, psF)[(tt * 2 + hh) % 2]
                    bwait("tensor", b)
                    tok = 0
                    for i in range(NIC):
                        tok = inc("tensor", lambda e, i=i, tt=tt, hh=hh, b=b:
                                  e.matmul(b[:, :],
                                           act[:, i, tt * 128:(tt + 1) * 128],
                                           dwe[:, i, hh * 512:(hh + 1) * 512],
                                           start=(i == 0),
                                           stop=(i == NIC - 1)), tMM)
                    wait("vector", tMM, tok)
                    if ei == 0:
                        last_r2 = inc("vector", lambda e, gtt=gtt, hh=hh, b=b:
                                      e.tensor_tensor(
                                          R2[:, gtt, hh * 512:(hh + 1) * 512],
                                          R2[:, gtt, hh * 512:(hh + 1) * 512],
                                          b[:, :], op=ALU.add), vA)
                    else:
                        last_r2 = inc("vector", lambda e, gtt=gtt, hh=hh, b=b,
                                      ei=ei: e.scalar_tensor_tensor(
                                          R2[:, gtt, hh * 512:(hh + 1) * 512],
                                          b[:, :], wrt[:, gtt, ei - 1:ei],
                                          R2[:, gtt, hh * 512:(hh + 1) * 512],
                                          op0=ALU.mult, op1=ALU.add), vA)
                    bset(b, vA, last_r2)
                    if ei == NE and hh == 1:
                        wait("sync", vA, last_r2)
                        inc("sync", lambda e, gtt=gtt: e.dma_start(
                            out=out_d[gtt * 128:(gtt + 1) * 128, :],
                            in_=R2[:, gtt, :]), dO, 16)
            act_free_mm = tMM.n
        dwe_rel = tMM.n
        if upto == ("ffn%d" % ei):
            dump_and_stop()

    if not dbg["stopped"]:
        wait("sync", dO, 16 * NTT)

    with nc.Block() as block:
        @block.sync
        def _(eng):
            for f in prog["sync"]:
                f(eng)

        @block.gpsimd
        def _(eng):
            for f in prog["gpsimd"]:
                f(eng)

        @block.tensor
        def _(eng):
            for f in prog["tensor"]:
                f(eng)

        @block.vector
        def _(eng):
            for f in prog["vector"]:
                f(eng)

        @block.scalar
        def _(eng):
            for f in prog["scalar"]:
                f(eng)

    return nc


def kernel(L0_attn_x, L0_attn_Wq, L0_attn_Wk, L0_attn_Wv, L0_ffn_shared_gate,
           L0_ffn_shared_up, L0_ffn_shared_down, L0_ffn_gate_w, L0_ffn_up_w,
           L0_ffn_down_w, L0_ffn_router_w, L0_ffn_routing_bias):
    x = np.ascontiguousarray(np.asarray(L0_attn_x, dtype=np.float32))
    wq = np.ascontiguousarray(np.asarray(L0_attn_Wq, dtype=np.float32))
    wk = np.ascontiguousarray(np.asarray(L0_attn_Wk, dtype=np.float32))
    wv = np.ascontiguousarray(np.asarray(L0_attn_Wv, dtype=np.float32))
    sgwT = np.ascontiguousarray(np.asarray(L0_ffn_shared_gate, dtype=np.float32).T)
    suwT = np.ascontiguousarray(np.asarray(L0_ffn_shared_up, dtype=np.float32).T)
    sdwT = np.ascontiguousarray(np.asarray(L0_ffn_shared_down, dtype=np.float32).T)
    gw = np.ascontiguousarray(np.asarray(L0_ffn_gate_w, dtype=np.float32))
    uw = np.ascontiguousarray(np.asarray(L0_ffn_up_w, dtype=np.float32))
    dw = np.ascontiguousarray(np.asarray(L0_ffn_down_w, dtype=np.float32))
    rwT = np.ascontiguousarray(np.asarray(L0_ffn_router_w, dtype=np.float32).T)
    rbias = np.ascontiguousarray(np.broadcast_to(
        np.asarray(L0_ffn_routing_bias, dtype=np.float32), (128, NE)))
    ident = np.eye(128, dtype=np.float32)
    pp = np.arange(128)
    dmask = np.where(pp[None, :] <= pp[:, None], 0.0, NEG_BIG).astype(np.float32)

    nc = build_program()
    in_maps = []
    for c in range(8):
        in_maps.append({
            "x": np.ascontiguousarray(x[c]), "wq": wq, "wk": wk, "wv": wv,
            "sgwT": sgwT, "suwT": suwT, "sdwT": sdwT, "gw": gw, "uw": uw,
            "dw": dw, "rwT": rwT, "rbias": rbias, "ident": ident,
            "dmask": dmask,
        })
    res = run_bass_kernel_spmd(nc, in_maps, core_ids=list(range(8)))
    out = np.stack([res.results[c]["out"] for c in range(8)], axis=0)
    return out.astype(np.float32)
